# version 21 (final)
# baseline (speedup 1.0000x reference)
import numpy as np
import ml_dtypes
import concourse.bass as bass
import concourse.mybir as mybir
from concourse.bass_utils import run_bass_kernel_spmd

F32 = mybir.dt.float32
BF16 = mybir.dt.bfloat16
AF = mybir.ActivationFunctionType
ALU = mybir.AluOpType
AX = mybir.AxisListType

P = 128
T = 2048
NT = 16
D = 1024
DC = 8
H = 16
LHN = 8
FF = 4096
FC = 32
NQT = 8
NHT = 10
GT = 512
NG = 4
EPS = 1e-5
SBUF_LIMIT = 229376

ENGINES = ("sync", "tensor", "vector", "scalar", "gpsimd")
NDMASEM = 8
CH_STREAM = {"tensor": "tensor", "vector": "vector", "scalar": "scalar",
             "pool": "gpsimd", "cc": "gpsimd"}
for _i in range(NDMASEM):
    CH_STREAM[f"syncd{_i}"] = "sync"
for _i in range(2):
    CH_STREAM[f"poold{_i}"] = "gpsimd"
COMPUTE_CH = ("tensor", "vector", "scalar", "pool")


class Plan:
    def __init__(self):
        self.ops = {e: [] for e in ENGINES}
        self.cnt = {c: 0 for c in CH_STREAM}
        self.waited = {e: {} for e in ENGINES}

    def emit(self, stream, fn, waits=(), channel=None, inc=0):
        waits = tmax(*waits)
        ws = []
        for (ch, v) in waits:
            if v <= 0:
                continue
            assert v <= self.cnt[ch], f"wait on future {ch}:{v} > {self.cnt[ch]}"
            if ch in COMPUTE_CH and CH_STREAM[ch] == stream and v < self.cnt[ch]:
                continue
            if self.waited[stream].get(ch, 0) >= v:
                continue
            self.waited[stream][ch] = v
            ws.append((ch, v))
        tok = None
        if inc:
            self.cnt[channel] += inc
            tok = (channel, self.cnt[channel])
        self.ops[stream].append((ws, fn, channel, inc))
        return tok


def tmax(*toks):
    out = {}

    def add(t):
        if t is None:
            return
        if isinstance(t, list):
            for it in t:
                add(it)
        else:
            c, v = t
            out[c] = max(out.get(c, 0), v)

    for t in toks:
        add(t)
    return list(out.items())


class Mem:
    def __init__(self, nc):
        self.nc = nc
        self.cur = int(nc.sbuf_base)
        self.limit = int(nc.sbuf_top)
        self.peak = 0

    def alloc(self, name, shape, dtype, at=None):
        per_part = int(np.prod(shape[1:])) * mybir.dt.size(dtype)
        addr = self.cur if at is None else at
        addr = (addr + 31) & ~31
        end = addr + per_part
        if at is None:
            self.cur = end
        self.peak = max(self.peak, end)
        assert end <= self.limit, f"SBUF overflow at {name}: {end} > {self.limit}"
        return self.nc.alloc_sbuf_tensor_at(name, list(shape), dtype, offset=addr)


def build_nc(debug=False, stage=3):
    nc = bass.Bass("TRN2", target_bir_lowering=False)
    p = Plan()
    mem = Mem(nc)
    dbg_outs = []

    def dram_in(name, shape, dtype=F32):
        return nc.dram_tensor(name, shape, dtype, kind="ExternalInput").ap()

    xin = dram_in("xin", [T + 256, D])
    wq_d = dram_in("wq", [P, DC * 1024]); wk_d = dram_in("wk", [P, DC * 1024])
    wv_d = dram_in("wv", [P, DC * 1024])
    wgq_d = dram_in("wgq", [P, 4 * 1024]); wgk_d = dram_in("wgk", [P, 4 * 1024])
    wgv_d = dram_in("wgv", [D, 512])
    rout_d = dram_in("r_out", [P, 4 * D])
    wout_d = dram_in("w_out", [D, D])
    gwbd_d = dram_in("gwbd", [D, 32])
    w1t_d = dram_in("w1t", [P, FC * 1024])
    w2t_d = dram_in("w2t", [P, FC * 1024])
    b1r_d = dram_in("b1r", [P, FC])
    b2r_d = dram_in("b2r", [P, DC])
    maskt_d = dram_in("maskt", [P, 1280], BF16)
    lmask_d = dram_in("lmask", [NT * P, 512], BF16)
    identb_d = dram_in("identb", [P, P], BF16)
    identf_d = dram_in("identf", [P, P], F32)
    onesb_d = dram_in("onesb", [P, 1], BF16)
    onesf_d = dram_in("onesf", [P, P], F32)
    poolv_d = dram_in("poolv", [P, 1], BF16)

    out_d = nc.dram_tensor("out", [T, D], F32, kind="ExternalOutput").ap()
    pool_src = nc.dram_tensor("pool_src", [D, 16], F32).ap()
    pool_dst = nc.dram_tensor("pool_dst", [8 * D, 16], F32, addr_space="Shared").ap()
    w1tern_d = nc.dram_tensor("w1tern", [P, FC * 1024], BF16).ap()
    xsave_d = nc.dram_tensor("xsave_d", [P, 2048], BF16).ap()
    w2tern_d = nc.dram_tensor("w2tern", [P, FC * 1024], BF16).ap()

    xres = mem.alloc("xres", [P, NT * D], F32)
    stg = mem.alloc("stg", [P, 2 * 1024], F32)
    identb = mem.alloc("identb", [P, P], BF16)
    identf = mem.alloc("identf", [P, P], F32)
    onesb = mem.alloc("onesb", [P, 1], BF16)
    onesf = mem.alloc("onesf", [P, P], F32)
    poolv = mem.alloc("poolv", [P, 1], BF16)
    smv = mem.alloc("smv", [P, 192], F32)
    epsc = mem.alloc("epsc", [P, 1], F32)
    b1c = mem.alloc("b1c", [P, FC], F32)
    b2c = mem.alloc("b2c", [P, DC], F32)
    tgA = mem.alloc("tgA", [P, 1024], BF16)
    tgB = mem.alloc("tgB", [P, 1024], BF16)
    UNION = mem.cur

    kfm = mem.alloc("kfm", [P, DC * 1280], BF16)
    vtm = mem.alloc("vtm", [P, NHT * D], BF16)
    qfm = mem.alloc("qfm", [P, DC * 1024], BF16)
    gqfm = mem.alloc("gqfm", [P, 4 * 1024], BF16)
    pooledb = mem.alloc("pooledb", [P, DC * 64], BF16)
    gkfm = mem.alloc("gkfm", [P, 4 * 64], BF16)
    gvtm = mem.alloc("gvtm", [P, 512], BF16)
    maskt = mem.alloc("maskt_sb", [P, 1280], BF16)

    gwbdb = mem.alloc("gwbdb", [P, 256], BF16)
    routb = mem.alloc("routb", [P, 4 * D], BF16)
    woutb = mem.alloc("woutb", [P, DC * D], BF16)
    wcolb = [mem.alloc(f"wcolb{i}", [P, 1024], BF16) for i in range(2)]
    expg = mem.alloc("expg", [P, 32], F32)
    gsig = mem.alloc("gsig", [P, 96], F32)
    recr = mem.alloc("recr", [P, 16], F32)
    rs8 = mem.alloc("rs8", [P, 16], F32)
    gstb = mem.alloc("gstb", [P, 128], BF16)
    OV = mem.cur
    xhalo = mem.alloc("xhalo", [P, 2 * D], F32)
    xlnfm = mem.alloc("xlnfm", [P, DC * 1280], BF16)
    xltmp = [mem.alloc("xltmp0", [P, D], BF16)] * 2
    vscr = [mem.alloc("vscr0", [P, 512], BF16)] * 2
    ATTN_END = mem.cur
    mem.cur = UNION
    pooled_g = [mem.alloc(f"pooled_g{i}", [P, 512], F32) for i in range(2)]
    pooled_f = mem.alloc("pooled_f", [P, 512], F32)
    pooled_own = mem.alloc("pooled_own", [P, 128], F32)
    bselb = mem.alloc("bselb", [P, 1], F32)
    mem.cur = OV
    etb = [mem.alloc(f"etb{i}", [P, 768], BF16) for i in range(2)]
    elb = mem.alloc("elb", [P, 512], BF16)
    elmt = mem.alloc("elmt", [P, 1024], BF16)
    remfm = mem.alloc("remfm", [P, 512], BF16)
    gated_b = [mem.alloc(f"gated_b{i}", [P, D], BF16) for i in range(3)]
    gatedt = mem.alloc("gatedt", [P, D], BF16)

    localsb = mem.alloc("localsb", [P, D], BF16)
    remsb = mem.alloc("remsb", [P, D], BF16)
    exp16 = mem.alloc("exp16", [P, 1024], BF16)
    lmaskw = mem.alloc("lmaskw", [P, 512], BF16)
    aotmp = [mem.alloc(f"aotmp{i}", [P, D], F32) for i in range(2)]
    assert mem.cur <= ATTN_END, f"loop overlay too big: {mem.cur} vs {ATTN_END}"



    mem.cur = UNION + 4096
    HB0 = (mem.cur + 31) & ~31
    hbuf = [mem.alloc(f"hbuf{i}", [P, FC * 512], BF16) for i in range(2)]
    HB1 = HB0 + FC * 512 * 2
    tstg = [mem.alloc(f"tstg{i}", [P, 1024], F32, at=HB1 + i * 4096) for i in range(4)]
    tsa = [mem.alloc(f"tsa{i}", [P, 1024], BF16, at=HB1 + 16384 + i * 2048) for i in range(2)]
    tsb = [mem.alloc(f"tsb{i}", [P, 1024], BF16, at=HB1 + 20480 + i * 2048) for i in range(2)]
    fdump = mem.alloc("fdump", [P, 1024], BF16, at=HB1 + 24576)
    xln3fm = [mem.alloc("xln3fm0", [P, DC * 512], BF16)] * 2
    w1colb = [mem.alloc(f"w1colb{i}", [P, 1024], BF16) for i in range(4)]
    w2colb = [mem.alloc(f"w2colb{i}", [P, FC * 128], BF16) for i in range(2)]
    xdelta = [mem.alloc(f"xdelta{i}", [P, 512], F32) for i in range(2)]
    ffn_tm = mem.alloc("ffn_tm", [P, 4 * D], F32)
    ln3tmp = mem.alloc("ln3tmp", [P, 1024], BF16)
    ftmp = mem.alloc("ftmp", [P, D], F32)
    print("FFN region end:", mem.cur, "attn end:", ATTN_END, "limit:", mem.limit)
    assert mem.cur <= max(ATTN_END, mem.cur) and mem.cur <= mem.limit

    psA = nc.alloc_psum_tensor("psA", [P, 1024], F32)
    psB = nc.alloc_psum_tensor("psB", [P, 1024], F32)
    psL = nc.alloc_psum_tensor("psL", [P, 1024], F32)
    psM = nc.alloc_psum_tensor("psM", [P, 512], F32)
    psT = nc.alloc_psum_tensor("psT", [P, 1024], BF16)

    sems = {}

    dma_rot = [0]

    def dma(dst, src, waits=()):
        k = dma_rot[0] % NDMASEM
        dma_rot[0] += 1
        ch = f"syncd{k}"
        prior = [(ch, p.cnt[ch])]
        return p.emit("sync", lambda e, d=dst, s=src: e.dma_start(out=d, in_=s),
                      tmax(list(waits), prior), ch, 16)

    pdma_rot = [0]

    def dma_p(dst, src, waits=()):
        k = pdma_rot[0] % 2
        pdma_rot[0] += 1
        ch = f"poold{k}"
        prior = [(ch, p.cnt[ch])]
        return p.emit("gpsimd", lambda e, d=dst, s=src: e.dma_start(out=d, in_=s),
                      tmax(list(waits), prior), ch, 16)

    def t_op(fn, waits=()):
        return p.emit("tensor", fn, waits, "tensor", 1)

    def _chained(stream, ch, fn, waits):
        prior = [(ch, p.cnt[ch])]
        return p.emit(stream, fn, tmax(list(waits), prior), ch, 1)

    def v_op(fn, waits=()):
        return _chained("vector", "vector", fn, waits)

    def s_op(fn, waits=()):
        return _chained("scalar", "scalar", fn, waits)

    def g_op(fn, waits=()):
        return _chained("gpsimd", "pool", fn, waits)

    def bar():
        chs = list(COMPUTE_CH) + [f"syncd{i}" for i in range(NDMASEM)] + \
            [f"poold{i}" for i in range(2)]
        return [(c, p.cnt[c]) for c in chs]

    def tap(name, ap, waits):
        if not debug:
            return None
        dd = nc.dram_tensor("tap_" + name, [int(ap.shape[0]), int(np.prod(ap.shape[1:]))],
                            ap.dtype, kind="ExternalOutput").ap()
        dbg_outs.append("tap_" + name)
        flat = ap
        return dma(dd, flat, waits)

    cast_busy = [None, None]
    cast_buf = [0]

    def _stg_view(b, src):
        n = int(np.prod(src.shape[1:]))
        assert n <= 1024, n
        npart = int(src.shape[0])
        flat = stg[0:npart, b * 1024: b * 1024 + n]
        if len(src.shape) == 3:
            shaped = flat.rearrange("q (a b) -> q a b", a=int(src.shape[1]))
        else:
            shaped = flat
        return flat, shaped

    def stream_cast(dst, src, extra_waits=None):
        b = cast_buf[0]
        cast_buf[0] ^= 1
        flat, shaped = _stg_view(b, src)
        tok_d = dma(shaped, src, waits=tmax(cast_busy[b], extra_waits))
        tok_c = v_op(lambda e, sv=flat, dd=dst: e.tensor_copy(out=dd, in_=sv), [tok_d])
        cast_busy[b] = tok_c
        return tok_c

    def stage_load(src, cols, extra_waits=None):
        b = cast_buf[0]
        cast_buf[0] ^= 1
        flat, shaped = _stg_view(b, src)
        tok_d = dma(shaped, src, waits=tmax(cast_busy[b], extra_waits))
        return b, flat, tok_d

    tok_eps = v_op(lambda e: e.memset(epsc[:], EPS))
    tok_idb = dma(identb[:], identb_d)
    tok_idf = dma(identf[:], identf_d)
    tok_onb = dma(onesb[:], onesb_d)
    tok_onf = dma(onesf[:], onesf_d)
    tok_pv = dma(poolv[:], poolv_d)
    tok_b1 = dma(b1c[:], b1r_d)
    tok_b2 = dma(b2c[:], b2r_d)
    tok_bs = dma(bselb[:], dram_in("bsel", [P, 1]))
    tok_mask = dma(maskt[:], maskt_d)
    exp16_d = dram_in("exp16", [P, 1024], BF16)
    tok_x = dma(xres.ap().rearrange("q (t d) -> q t d", d=D),
                xin[256:, :].rearrange("(t q) d -> q t d", q=P))
    tok_xh = dma(xhalo.ap().rearrange("q (t d) -> q t d", d=D),
                 xin[0:256, :].rearrange("(t q) d -> q t d", q=P))

    tok_gwbd = stream_cast(gwbdb[:], gwbd_d.rearrange("(c q) g -> q c g", q=P))
    tok_rout = tmax(*[stream_cast(routb[:, i * 1024:(i + 1) * 1024],
                                  rout_d[:, i * 1024:(i + 1) * 1024])
                      for i in range(4)])
    tok_wout = tmax(*[stream_cast(woutb[:, i * 1024:(i + 1) * 1024],
                                  wout_d.rearrange("(c q) d -> q c d", q=P)[:, i:i + 1, :])
                      for i in range(8)])

    tok_z = v_op(lambda e: e.memset(smv[:, 114:116], 0.0))
    v_op(lambda e: e.memset(gsig[:], 0.0))
    abs_state = {"i": 0, "prev": {0: tok_z, 1: tok_z}, "toks": [],
                 "pbar": None, "pfree": [None, None]}
    tern_state = {"i": 0, "w1c": [None] * FC, "w2c": {}, "w1s": [None] * FC, "w2s": {},
                  "thr": None, "all": [],
                  "stg_free": [None] * 4, "sa_free": [None, None], "sb_free": [None, None],
                  "bar": None}

    def emit_tern(n):
        if tern_state["thr"] is None:
            return
        while n > 0 and tern_state["i"] < 64:
            i = tern_state["i"]
            tern_state["i"] += 1
            n -= 1
            ss = i % 4
            sp = i % 2
            tok_thr2 = tern_state["thr"]
            if i < 32:
                fc = i
                src_ap = w1t_d[:, fc * 1024:(fc + 1) * 1024]
                dst_ap = w1tern_d[:, fc * 1024:(fc + 1) * 1024]
                b0, b1 = smv[:, 146:147], smv[:, 147:148]
                wdst = w1colb[fc % 4][:]
                wread = tok_w1_read[fc % 4]
            else:
                j = i - 32
                dc, hh = j // 4, j % 4
                src_ap = w2t_d[:, (dc * 4 + hh) * 1024:(dc * 4 + hh + 1) * 1024]
                dst_ap = w2tern_d[:, (dc * 4 + hh) * 1024:(dc * 4 + hh + 1) * 1024]
                b0, b1 = smv[:, 148:149], smv[:, 149:150]
                wdst = w2colb[dc % 2][:, hh * 1024:(hh + 1) * 1024]
                wread = tok_w2_read[dc % 2]
            tok_d = dma(tstg[ss][:], src_ap,
                        waits=tmax(tern_state["stg_free"][ss], tok_thr2, tern_state["bar"]))
            tok_sa = s_op(lambda e, ss=ss, sp=sp, b0=b0: e.activation(
                tsa[sp][:], tstg[ss][:], AF.Sign, bias=b0, scale=1.0),
                tmax([tok_d], tok_thr2, tern_state["sa_free"][sp]))
            tok_sb = s_op(lambda e, ss=ss, sp=sp, b1=b1: e.activation(
                tsb[sp][:], tstg[ss][:], AF.Sign, bias=b1, scale=1.0),
                tmax(tern_state["sb_free"][sp]))
            tern_state["stg_free"][ss] = [tok_sb]
            tok_add = v_op(lambda e, sp=sp, wdst=wdst: e.tensor_tensor(
                wdst, tsa[sp][:], tsb[sp][:], ALU.add), tmax([tok_sb], wread))
            tern_state["sa_free"][sp] = [tok_add]
            tern_state["sb_free"][sp] = [tok_add]
            tok_cw = dma_p(dst_ap, wdst, waits=[tok_add])
            tern_state["all"].append(tok_cw)
            if i < 32:
                tern_state["w1c"][i] = tok_cw
                tern_state["w1s"][i] = [tok_add]
            else:
                j = i - 32
                tern_state["w2c"].setdefault(j // 4, []).append(tok_cw)
                tern_state["w2s"].setdefault(j // 4, []).append(tok_add)

    def emit_thresholds():
        nonlocal tok_psL_free
        tok_abs = tmax(*abs_state["toks"])
        assert abs_state["i"] == 64
        thm = t_op(lambda e: e.matmul(psL[:, 0:2], onesf[:], smv[:, 114:116],
                                      start=True, stop=True),
                   tmax(tok_abs, tok_psL_free, [tok_onf]))
        tok_thv = v_op(lambda e: e.tensor_scalar(smv[:, 164:166], psL[:, 0:2],
                                                 1.0 / (D * FF), None, ALU.mult), [thm])
        tok_psL_free = [tok_thv]
        v_op(lambda e: e.tensor_scalar(smv[:, 164:166], smv[:, 164:166], 1e-8, None, ALU.max))
        v_op(lambda e: e.tensor_scalar(smv[:, 147:148], smv[:, 164:165], 0.5, None, ALU.mult))
        v_op(lambda e: e.tensor_scalar(smv[:, 149:150], smv[:, 165:166], 0.5, None, ALU.mult))
        v_op(lambda e: e.tensor_scalar(smv[:, 146:147], smv[:, 164:165], -0.5, None, ALU.mult))
        tok_t = v_op(lambda e: e.tensor_scalar(smv[:, 148:149], smv[:, 165:166],
                                               -0.5, None, ALU.mult))
        tern_state["thr"] = [tok_t]
        return tok_thv

    def emit_abs(n):
        while n > 0 and abs_state["i"] < 64:
            i = abs_state["i"]
            abs_state["i"] += 1
            n -= 1
            if i < 32:
                src_ap = w1t_d[:, i * 1024:(i + 1) * 1024]
                pc = 114
            else:
                j = i - 32
                src_ap = w2t_d[:, j * 1024:(j + 1) * 1024]
                pc = 115
            scr = 116 + (i % 2)
            if abs_state["pbar"] is None:
                b, stg_v, tok_d = stage_load(src_ap, 1024)
            else:
                b = None
                stg_v = xhalo[:, (i % 2) * 1024:(i % 2 + 1) * 1024]
                tok_d = dma_p(stg_v, src_ap,
                              waits=tmax(abs_state["pfree"][i % 2], abs_state["pbar"]))
            tok_a = s_op(lambda e, sv=stg_v, scr=scr: e.activation(
                tgA[:, 0:1024], sv, AF.Abs,
                bias=0.0, scale=1.0, accum_out=smv[:, scr:scr + 1]),
                tmax([tok_d], [abs_state["prev"][i % 2]]))
            tok_add = v_op(lambda e, pc=pc, scr=scr: e.tensor_tensor(
                smv[:, pc:pc + 1], smv[:, pc:pc + 1], smv[:, scr:scr + 1], ALU.add),
                [tok_a])
            if b is not None:
                cast_busy[b] = tok_a
            else:
                abs_state["pfree"][i % 2] = [tok_a]
            abs_state["prev"][i % 2] = tok_add
            abs_state["toks"].append(tok_add)


    def xtile(i):
        return xres[:, i * D:(i + 1) * D] if i < NT else xhalo[:, (i - NT) * D:(i - NT + 1) * D]

    red_toks = []
    for i in range(18):
        w = [tok_x] if i < NT else [tok_xh]
        red_toks.append(v_op(lambda e, i=i: e.tensor_reduce(
            smv[:, 36 + i:37 + i], xtile(i), AX.X, ALU.add), w))
        red_toks.append(s_op(lambda e, i=i: e.activation(
            tgA[:, 0:D], xtile(i), AF.Square, bias=0.0, scale=1.0,
            accum_out=smv[:, 54 + i:55 + i]), w))
    tok_red = tmax(*red_toks)
    v_op(lambda e: e.tensor_scalar(smv[:, 0:18], smv[:, 36:54], -1.0 / D, None, ALU.mult), tok_red)
    v_op(lambda e: e.tensor_scalar(smv[:, 54:72], smv[:, 54:72], 1.0 / D, None, ALU.mult), tok_red)
    v_op(lambda e: e.tensor_tensor(smv[:, 72:90], smv[:, 0:18], smv[:, 0:18], ALU.mult))
    tok_var = v_op(lambda e: e.tensor_tensor(smv[:, 54:72], smv[:, 54:72], smv[:, 72:90], ALU.subtract))
    s_op(lambda e: e.activation(smv[:, 18:36], smv[:, 54:72], AF.Ln, bias=epsc[:], scale=1.0), [tok_var])
    tok_stats = s_op(lambda e: e.activation(smv[:, 18:36], smv[:, 18:36], AF.Exp, bias=0.0, scale=-0.5))

    def ln1_apply(dst, i, waits):
        return v_op(lambda e, i=i, dst=dst: e.tensor_scalar(
            dst, xtile(i), smv[:, i:i + 1], smv[:, 18 + i:19 + i], ALU.add, ALU.mult), waits)

    pool_toks = []
    prev_pool = None
    for i in range(NT):
        tok_ap = ln1_apply(xltmp[i % 2][:], i, tmax([tok_stats], prev_pool))
        mm = []
        for dc in range(DC):
            mm.append(t_op(lambda e, i=i, dc=dc: e.matmul(
                psM[:, 64 + dc * 16 + i: 65 + dc * 16 + i],
                xltmp[i % 2][:, dc * 128:(dc + 1) * 128], poolv[:],
                start=True, stop=True), tmax([tok_ap], [tok_pv])))
        prev_pool = tmax(*mm)
        pool_toks.append(prev_pool)
        emit_abs(2)
    tok_poole = v_op(lambda e: e.tensor_copy(out=pooled_own[:], in_=psM[:, 64:192]),
                     tmax(*pool_toks))
    tok_ps = dma(pool_src.rearrange("(c q) n -> q c n", q=P),
                 pooled_own.ap().rearrange("q (c n) -> q c n", n=16), waits=[tok_poole])
    tok_cc = p.emit("gpsimd", lambda e: e.collective_compute(
        "AllGather", ALU.bypass, replica_groups=[[0, 1, 2, 3, 4, 5, 6, 7]],
        ins=[pool_src[:, :]], outs=[pool_dst[:, :]]), [tok_ps], "cc", 1)
    wcol_use = [None, None]
    gates_tok = {}

    def emit_gates_weights():
        nonlocal tok_psA_free, tok_psM_free
        tok_pga = tmax(*[dma(
            pooled_g[b].ap().rearrange("q (c n) -> q c n", c=8)[:, :, r * 16:(r + 1) * 16],
            pool_dst[(4 * b + r) * 1024:(4 * b + r + 1) * 1024, :]
            .rearrange("(c q) n -> q c n", q=P),
            waits=[tok_cc]) for r in range(4) for b in range(2)])
        v_op(lambda e: e.tensor_tensor(pooled_f[:], pooled_g[1][:], pooled_g[0][:], ALU.subtract),
             tmax(tok_pga, [tok_bs]))
        tok_psel = v_op(lambda e: e.scalar_tensor_tensor(
            pooled_f[:], pooled_f[:], bselb[:], pooled_g[0][:], ALU.mult, ALU.add))
        tok_poolf = g_op(lambda e: e.tensor_copy(out=pooledb[:], in_=pooled_f[:]), [tok_psel])
        tap("pooledb", pooledb[:], [tok_poolf])

        tok_gk_mm = []
        for gc in range(4):
            tok_w = stream_cast(wcolb[gc % 2][:],
                                wgk_d[:, gc * 1024:(gc + 1) * 1024],
                                extra_waits=wcol_use[gc % 2])
            mm = None
            for dc in range(DC):
                mm = t_op(lambda e, gc=gc, dc=dc: e.matmul(
                    psM[:, 256 + gc * 64: 256 + (gc + 1) * 64],
                    wcolb[gc % 2][:, dc * 128:(dc + 1) * 128],
                    pooledb[:, dc * 64:(dc + 1) * 64],
                    start=(dc == 0), stop=(dc == DC - 1)),
                    tmax([tok_w], [tok_poolf]))
            wcol_use[gc % 2] = mm
            tok_gk_mm.append(mm)
        tok_gke = s_op(lambda e: e.activation(gkfm[:], psM[:, 256:512], AF.Copy, bias=0.0, scale=1.0),
                       tmax(*tok_gk_mm))
        tap("gkfm", gkfm[:], [tok_gke])

        tok_gv_mm = None
        for dc in range(DC):
            tok_w = stream_cast(wcolb[dc % 2][:, 0:512],
                                wgv_d.rearrange("(c q) g -> q c g", q=P)[:, dc:dc + 1, :],
                                extra_waits=wcol_use[dc % 2])
            tok_gv_mm = t_op(lambda e, dc=dc: e.matmul(
                psA[0:64, 0:512], pooledb[:, dc * 64:(dc + 1) * 64],
                wcolb[dc % 2][:, 0:512],
                start=(dc == 0), stop=(dc == DC - 1)),
                tmax([tok_w], [tok_poolf], [tok_gke], tok_psA_free))
            wcol_use[dc % 2] = tok_gv_mm
        tok_gv = v_op(lambda e: e.tensor_copy(out=gvtm[0:64, :], in_=psA[0:64, 0:512]), [tok_gv_mm])
        tap("gvtm", gvtm[0:64, :], [tok_gv])

        gates_tok["poolf"] = tok_poolf
        gates_tok["gke"] = tok_gke
        gates_tok["gv"] = tok_gv
        tok_psM_free = [tok_gke, tok_poole]
        tok_psA_free = [tok_gv]

    tok_psA_free = [tok_poole]
    tok_psB_free = [None]
    tok_psL_free = [None]
    tok_psM_free = [tok_poole]
    tok_psT_free = [None]
    tok_exp_sl = [None, None]
    tok_mask_sl = [None, None]
    tok_pv_sl = [None, None]
    tok_gb_read = [None, None, None]
    tok_elb_read = None
    tok_elnb_read = None
    tok_elmt_read = None
    tok_remfm_read = None
    tok_gatedt_read = None
    tok_expg_w = None
    tok_gsig_r = None
    tok_prev_kv_read = None
    tok_loopbuf_read = bar()
    tok_ln2_done = None
    tok_xsave = None
    tok_localsb_read = None
    tok_remsb_read = None
    tok_ao_read = [None, None]
    tok_stt_q = [None, None, None]
    tok_lmw_read = None

    if stage >= 10:
        n_halves = 1
    else:
        n_halves = 2 if stage >= 2 else (1 if stage >= 1 else 0)
    for half in range(n_halves):
        xln_toks = []
        prev_tp = tmax(tok_psT_free)
        j0 = 0
        if half == 1:
            tok_hcp = dma(xlnfm.ap().rearrange("q (c j) -> q c j", j=1280)[:, :, 0:256],
                          xsave_d.rearrange("q (c j) -> q c j", j=256),
                          waits=tmax(tok_loopbuf_read, tok_prev_kv_read, [tok_xsave]))
            xln_toks.append(tok_hcp)
            j0 = 2
        for j in range(j0, NHT):
            gi = half * NQT + j - 2
            si = gi if gi >= 0 else NT + (gi + 2)
            tok_ap = ln1_apply(xltmp[j % 2][:], si,
                               tmax([tok_stats], prev_tp, tok_loopbuf_read, tok_prev_kv_read))
            tps = []
            for dc in range(DC):
                tps.append(t_op(lambda e, j=j, dc=dc: e.transpose(
                    psT[:, dc * 128:(dc + 1) * 128],
                    xltmp[j % 2][:, dc * 128:(dc + 1) * 128], identb[:]),
                    tmax([tok_ap], prev_tp, [tok_idb])))
            dst = xlnfm.ap().rearrange("q (c j) -> q c j", j=1280)[:, :, j * 128:(j + 1) * 128]
            tok_ev = v_op(lambda e, dst=dst: e.tensor_copy(
                out=dst, in_=psT.ap().rearrange("q (c j) -> q c j", j=128)), tmax(*tps))
            prev_tp = tok_ev
            xln_toks.append(tok_ev)
        tok_xln = tmax(*xln_toks)
        tok_psT_free = [tok_xln]
        if half == 0:
            tok_xsave = dma(xsave_d.rearrange("q (c j) -> q c j", j=256),
                            xlnfm.ap().rearrange("q (c j) -> q c j", j=1280)[:, :, 1024:1280],
                            waits=tmax(tok_xln))
            tok_xln = tmax(tok_xln, [tok_xsave])
            abs_state["pbar"] = tmax(tok_xln)
            emit_gates_weights()
            tap("xlnfm", xlnfm[:], tmax([tok_xln]))

        def proj(w_dram, n_outc, span, scale, dst_fn, own_only, evac_extra=None):
            nonlocal tok_psA_free, tok_psB_free
            toks = []
            rot = [0]
            for oc in range(n_outc):
                tok_w = stream_cast(wcolb[oc % 2][:],
                                    w_dram[:, oc * 1024:(oc + 1) * 1024],
                                    extra_waits=wcol_use[oc % 2])
                ncs = [(0, 512), (512, 512), (1024, 256)] if span == 1280 else [(0, 512), (512, 512)]
                for (c0, nn) in ncs:
                    use_a = (rot[0] % 2 == 0)
                    rot[0] += 1
                    ps = psA if use_a else psB
                    ps_free = tok_psA_free if use_a else tok_psB_free
                    mm = None
                    off = 256 if own_only else 0
                    for dc in range(DC):
                        mm = t_op(lambda e, oc=oc, c0=c0, nn=nn, dc=dc, ps=ps, off=off: e.matmul(
                            ps[:, 0:nn],
                            wcolb[oc % 2][:, dc * 128:(dc + 1) * 128],
                            xlnfm[:, dc * 1280 + off + c0: dc * 1280 + off + c0 + nn],
                            start=(dc == 0), stop=(dc == DC - 1)),
                            tmax([tok_w], [tok_xln], ps_free))
                    wcol_use[oc % 2] = mm
                    tok_e = s_op(lambda e, oc=oc, c0=c0, nn=nn, ps=ps, sc=scale: e.activation(
                        dst_fn(oc, c0, nn), ps[:, 0:nn], AF.Copy, bias=0.0, scale=sc),
                        tmax([mm], evac_extra))
                    if use_a:
                        tok_psA_free = [tok_e]
                    else:
                        tok_psB_free = [tok_e]
                    toks.append(tok_e)
                emit_abs(2)
            return tmax(*toks)

        tok_k = proj(wk_d, DC, 1280, 1.0,
                     lambda oc, c0, nn: kfm[:, oc * 1280 + c0: oc * 1280 + c0 + nn], False,
                     evac_extra=[gates_tok["poolf"]] if half == 0 else None)
        tok_q = proj(wq_d, DC, 1024, 0.125,
                     lambda oc, c0, nn: qfm[:, oc * 1024 + c0: oc * 1024 + c0 + nn], True)
        tok_gq = proj(wgq_d, 4, 1024, 0.125,
                      lambda oc, c0, nn: gqfm[:, oc * 1024 + c0: oc * 1024 + c0 + nn], True)

        v_toks = []
        tok_vscr_read = [None, None]
        for oc in range(DC):
            tok_w = stream_cast(wcolb[oc % 2][:],
                                wv_d[:, oc * 1024:(oc + 1) * 1024],
                                extra_waits=wcol_use[oc % 2])
            for ci, (c0, nn) in enumerate([(0, 512), (512, 512), (1024, 256)]):
                use_a = (ci % 2 == 0)
                ps = psA if use_a else psB
                ps_free = tok_psA_free if use_a else tok_psB_free
                mm = None
                for dc in range(DC):
                    mm = t_op(lambda e, oc=oc, c0=c0, nn=nn, dc=dc, ps=ps: e.matmul(
                        ps[:, 0:nn],
                        wcolb[oc % 2][:, dc * 128:(dc + 1) * 128],
                        xlnfm[:, dc * 1280 + c0: dc * 1280 + c0 + nn],
                        start=(dc == 0), stop=(dc == DC - 1)),
                        tmax([tok_w], [tok_xln], ps_free))
                wcol_use[oc % 2] = mm
                sv = ci % 2
                tok_e = s_op(lambda e, nn=nn, ps=ps, sv=sv: e.activation(
                    vscr[sv][:, 0:nn], ps[:, 0:nn], AF.Copy, bias=0.0, scale=1.0),
                    tmax([mm], tok_vscr_read[sv]))
                if use_a:
                    tok_psA_free = [tok_e]
                else:
                    tok_psB_free = [tok_e]
                nblk = nn // 128
                tps = []
                for i in range(nblk):
                    tps.append(t_op(lambda e, i=i, sv=sv: e.transpose(
                        psT[:, i * 128:(i + 1) * 128], vscr[sv][:, i * 128:(i + 1) * 128],
                        identb[:]), tmax([tok_e], tok_psT_free)))
                tok_vscr_read[sv] = tmax(*tps)
                v0 = c0 // 128
                dstv = vtm.ap().rearrange("q (v d) -> q v d", d=D)[:, v0:v0 + nblk,
                                                                  oc * 128:(oc + 1) * 128]
                srcv = psT.ap().rearrange("q (v d) -> q v d", d=128)[:, 0:nblk, :]
                tok_vv = v_op(lambda e, dstv=dstv, srcv=srcv: e.tensor_copy(out=dstv, in_=srcv),
                              tmax(*tps))
                tok_psT_free = [tok_vv]
                v_toks.append(tok_vv)
            emit_abs(2)
        tok_v = tmax(*v_toks)
        if half == 0:
            tap("kfm", kfm[:, 0:2048], tmax([tok_k]))
            tap("qfm", qfm[:, 0:2048], tmax([tok_q]))
            tap("gqfm", gqfm[:, 0:2048], tmax([tok_gq]))
            tap("vtm", vtm[:, 0:2048], tmax([tok_v]))

        tok_projs = tmax([tok_k], [tok_q], [tok_gq], [tok_v])
        if half == 0:
            emit_abs(64)
            emit_thresholds()
        kv_reads = []
        tok_exp16 = dma(exp16[:], exp16_d, waits=tmax(tok_loopbuf_read, tok_projs))

        if 100 <= stage < 200:
            nqt_run, sec = 1, stage - 100
        elif stage >= 200:
            nqt_run, sec = 1, 2
        else:
            nqt_run, sec = (NQT if stage < 10 else min(stage - 10, NQT)), 99

        def part2(qt):
            nonlocal tok_psT_free, tok_gatedt_read, tok_psL_free, tok_ln2_done
            g_tile = half * NQT + qt
            gts = []
            for dc in range(DC):
                gts.append(t_op(lambda e, dc=dc, qt=qt: e.transpose(
                    psT[:, dc * 128:(dc + 1) * 128], gated_b[qt % 3][:, dc * 128:(dc + 1) * 128],
                    identb[:]), tmax(tok_stt_q[qt % 3], tok_psT_free)))
            tok_gb_read[qt % 3] = tmax(*gts)
            tok_gte = s_op(lambda e: e.activation(gatedt[:], psT[:, 0:1024], AF.Copy,
                                                  bias=0.0, scale=1.0),
                           tmax(*gts, [tok_gatedt_read]))
            tok_psT_free = [tok_gte]
            wom = []
            for dc in range(DC):
                wom.append(t_op(lambda e, dc=dc: e.matmul(
                    psL[:, 0:512], gatedt[:, dc * 128:(dc + 1) * 128],
                    woutb[:, dc * 1024: dc * 1024 + 512],
                    start=(dc == 0), stop=(dc == DC - 1)),
                    tmax([tok_gte], tok_psL_free, tok_wout)))
                wom.append(t_op(lambda e, dc=dc: e.matmul(
                    psL[:, 512:1024], gatedt[:, dc * 128:(dc + 1) * 128],
                    woutb[:, dc * 1024 + 512: (dc + 1) * 1024],
                    start=(dc == 0), stop=(dc == DC - 1))))
            tok_wo = tmax(*wom)
            tok_gatedt_read = tok_wo
            c0 = 90 + 4 * (qt % 2)
            ao = aotmp[qt % 2]
            tok_aoe = s_op(lambda e, ao=ao: e.activation(
                ao[:], psL[:, 0:1024], AF.Copy, bias=0.0, scale=1.0),
                tmax(tok_wo, [tok_ao_read[qt % 2]]))
            tok_psL_free = [tok_aoe]
            tok_s2 = v_op(lambda e, c0=c0, ao=ao: e.tensor_reduce(
                smv[:, c0:c0 + 1], ao[:], AX.X, ALU.add), [tok_aoe])
            tok_q2 = s_op(lambda e, c0=c0, ao=ao: e.activation(
                gatedt[:, 0:1024], ao[:], AF.Square, bias=0.0, scale=1.0,
                accum_out=smv[:, c0 + 1:c0 + 2]), tmax([tok_aoe], [tok_s2]))
            v_op(lambda e, c0=c0: e.tensor_scalar(
                smv[:, c0 + 2:c0 + 3], smv[:, c0:c0 + 1], -1.0 / D, None, ALU.mult), [tok_q2])
            v_op(lambda e, c0=c0: e.tensor_scalar(
                smv[:, c0 + 1:c0 + 2], smv[:, c0 + 1:c0 + 2], 1.0 / D, None, ALU.mult))
            v_op(lambda e, c0=c0: e.tensor_tensor(
                smv[:, 159:160], smv[:, c0 + 2:c0 + 3], smv[:, c0 + 2:c0 + 3], ALU.mult))
            tok_v2 = v_op(lambda e, c0=c0: e.tensor_tensor(
                smv[:, c0 + 1:c0 + 2], smv[:, c0 + 1:c0 + 2], smv[:, 159:160], ALU.subtract))
            s_op(lambda e, c0=c0: e.activation(
                smv[:, c0 + 3:c0 + 4], smv[:, c0 + 1:c0 + 2], AF.Ln, bias=epsc[:], scale=1.0),
                [tok_v2])
            tok_r2 = s_op(lambda e, c0=c0: e.activation(
                smv[:, c0 + 3:c0 + 4], smv[:, c0 + 3:c0 + 4], AF.Exp, bias=0.0, scale=-0.5))
            tok_app = v_op(lambda e, c0=c0, ao=ao: e.tensor_scalar(
                ao[:], ao[:], smv[:, c0 + 2:c0 + 3], smv[:, c0 + 3:c0 + 4],
                ALU.add, ALU.mult), [tok_r2])
            tok_ln2 = g_op(lambda e, g=g_tile, ao=ao: e.tensor_tensor(
                xres[:, g * D:(g + 1) * D], xres[:, g * D:(g + 1) * D], ao[:], ALU.add),
                [tok_app])
            tok_ao_read[qt % 2] = tok_ln2
            tok_ln2_done = tok_ln2

        def emit_pair(hp, qt, g_tile, mvar, with_pv=True):
            sl = hp % 2
            ps = psA if sl == 0 else psB
            ps_free = tok_psA_free if sl == 0 else tok_psB_free
            smm = []
            for hl in range(2):
                for c in range(3):
                    smm.append(t_op(lambda e, hp=hp, hl=hl, c=c, qt=qt, ps=ps: e.matmul(
                        ps[:, hl * 512 + c * 128: hl * 512 + (c + 1) * 128],
                        kfm[hl * 64: hl * 64 + 64,
                            hp * 1280 + (qt + c) * 128: hp * 1280 + (qt + c + 1) * 128],
                        qfm[hl * 64: hl * 64 + 64,
                            hp * 1024 + qt * 128: hp * 1024 + (qt + 1) * 128],
                        start=True, stop=True),
                        tmax([tok_projs], ps_free, [tok_exp_sl[sl]])))
            tok_sc = tmax(*smm)
            kv_reads.append(tok_sc)
            tok_exp = tmax(
                s_op(lambda e, sl=sl, ps=ps: e.activation(
                    etb[sl].ap().rearrange("q (c z) -> q c z", z=128)[:, 0:6:2, :],
                    ps[:, 0:384], AF.Exp, bias=0.0, scale=1.0),
                    tmax(tok_sc, [tok_mask_sl[sl]], [tok_pv_sl[sl]])),
                s_op(lambda e, sl=sl, ps=ps: e.activation(
                    etb[sl].ap().rearrange("q (c z) -> q c z", z=128)[:, 1:6:2, :],
                    ps[:, 512:896], AF.Exp, bias=0.0, scale=1.0)))
            tok_exp_sl[sl] = tok_exp
            if sl == 0:
                tok_psA_free2[0] = [tok_exp]
            else:
                tok_psB_free2[0] = [tok_exp]
            ev = etb[sl].ap().rearrange("q (c z) -> q c z", z=256)
            mks = [
                v_op(lambda e, ev=ev, mv=mvar: e.tensor_tensor(
                    ev[:, 0:3:2, :], ev[:, 0:3:2, :],
                    maskt.ap().rearrange("q (c z) -> q c z", z=256)[:, 2 * mv:2 * mv + 2, :],
                    ALU.mult),
                    tmax([tok_exp], [tok_mask])),
            ]
            if g_tile == 0:
                mks.append(v_op(lambda e, sl=sl: e.tensor_tensor(
                    etb[sl][:, 256:512], etb[sl][:, 256:512],
                    maskt[:, 1024:1280], ALU.mult)))
            tok_mask_sl[sl] = tmax(*mks)
            if with_pv:
                emit_pv(hp, qt)

        def emit_pv(hp, qt):
            sl = hp % 2
            pvm = []
            for hl in range(2):
                h = 2 * hp + hl
                for c in range(3):
                    pvm.append(t_op(lambda e, sl=sl, hl=hl, h=h, c=c, qt=qt: e.matmul(
                        psL[:, h * 64:(h + 1) * 64],
                        etb[sl][:, c * 256 + hl * 128: c * 256 + (hl + 1) * 128],
                        vtm[:, (qt + c) * 1024 + h * 64: (qt + c) * 1024 + (h + 1) * 64],
                        start=(c == 0), stop=(c == 2)),
                        tmax(tok_mask_sl[sl], tok_psL_free, [tok_expg_w], tok_psM_free,
                             [tok_onb])))
                    pvm.append(t_op(lambda e, sl=sl, hl=hl, h=h, c=c: e.matmul(
                        psM[:, h:h + 1],
                        etb[sl][:, c * 256 + hl * 128: c * 256 + (hl + 1) * 128],
                        onesb[:],
                        start=(c == 0), stop=(c == 2))))
            tok_pv_sl[sl] = tmax(*pvm)
            kv_reads.append(tok_pv_sl[sl])
            tok_pv_all.append(tok_pv_sl[sl])

        for qt in range(nqt_run):
            g_tile = half * NQT + qt
            mvar = 0 if g_tile < 2 else 1
            tok_psA_free2 = [tok_psA_free]
            tok_psB_free2 = [tok_psB_free]

            gmm = None
            for dc in range(DC):
                gmm = t_op(lambda e, dc=dc, qt=qt: e.matmul(
                    psM[:, 16:48],
                    qfm[:, dc * 1024 + qt * 128: dc * 1024 + (qt + 1) * 128],
                    gwbdb[:, dc * 32:(dc + 1) * 32],
                    start=(dc == 0), stop=(dc == DC - 1)),
                    tmax([tok_projs], tok_psM_free, [tok_gwbd]))
            kv_reads.append(gmm)
            tok_expg_w = s_op(lambda e: e.activation(expg[:], psM[:, 16:48], AF.Exp,
                                                     bias=0.0, scale=-8.0),
                              tmax([gmm], [tok_gsig_r]))
            v_op(lambda e: e.tensor_scalar(gsig[:, 0:32], expg[:], 1.0, None, ALU.add),
                 [tok_expg_w])
            tok_gsig = v_op(lambda e: e.reciprocal(gsig[:, 0:32], gsig[:, 0:32]))

            tok_pv_all = []
            emit_pair(0, qt, g_tile, mvar, with_pv=False)
            emit_pair(1, qt, g_tile, mvar, with_pv=False)
            if qt > 1:
                part2(qt - 2)
            emit_pv(0, qt)
            emit_pv(1, qt)
            tok_psA_free = tok_psA_free2[0]
            tok_psB_free = tok_psB_free2[0]
            for hp in range(2, 8):
                emit_pair(hp, qt, g_tile, mvar, with_pv=True)
                tok_psA_free = tok_psA_free2[0]
                tok_psB_free = tok_psB_free2[0]
            tok_pv_last = tmax(*tok_pv_all)
            tok_psL_w = tok_pv_last

            lmm = []
            for lh in range(LHN):
                s_ = (lh % 2) * 4 + lh // 2
                lmm.append(t_op(lambda e, lh=lh, qt=qt, s_=s_: e.matmul(
                    psA[:, (lh % 2) * 512 + (lh // 2) * 64: (lh % 2) * 512 + (lh // 2) * 64 + 64],
                    gqfm[(lh % 2) * 64:(lh % 2) * 64 + 64,
                         (lh // 2) * 1024 + qt * 128: (lh // 2) * 1024 + (qt + 1) * 128],
                    gkfm[(lh % 2) * 64:(lh % 2) * 64 + 64, (lh // 2) * 64:(lh // 2 + 1) * 64],
                    start=True, stop=True),
                    tmax([tok_projs], tok_psA_free, tok_psB_free, [gates_tok["gke"]])))
            tok_lsc = tmax(*lmm)
            kv_reads.append(tok_lsc)
            tok_expl = tmax(
                s_op(lambda e: e.activation(elb[:, 0:256], psA[:, 0:256], AF.Exp,
                                            bias=0.0, scale=1.0),
                     tmax(tok_lsc, [tok_elb_read])),
                s_op(lambda e: e.activation(elb[:, 256:512], psA[:, 512:768], AF.Exp,
                                            bias=0.0, scale=1.0)))
            tok_lmw = dma(lmaskw[:], lmask_d[g_tile * P:(g_tile + 1) * P, :],
                          waits=[tok_lmw_read])
            mls = [v_op(lambda e: e.tensor_tensor(elb[:], elb[:], lmaskw[:], ALU.mult),
                        tmax([tok_expl], [tok_elnb_read], [tok_lmw]))]
            tok_lmw_read = mls[0]
            v_op(lambda e: e.tensor_reduce(
                rs8[:, 0:8], elb.ap().rearrange("q (l n) -> q l n", n=64), AX.X, ALU.add),
                tmax(*mls))
            v_op(lambda e: e.reciprocal(rs8[:, 8:16], rs8[:, 0:8]))
            nrm = []
            for s_ in range(LHN):
                nrm.append(v_op(lambda e, s_=s_: e.tensor_scalar(
                    elb[:, s_ * 64:(s_ + 1) * 64], elb[:, s_ * 64:(s_ + 1) * 64],
                    rs8[:, 8 + s_:9 + s_], None, ALU.mult)))
            tok_nrm = tmax(*nrm)
            tls = []
            for s_ in range(LHN):
                tls.append(t_op(lambda e, s_=s_: e.transpose(
                    psT[0:64, s_ * 128:(s_ + 1) * 128], elb[:, s_ * 64:(s_ + 1) * 64],
                    identb[:]), tmax(tok_nrm, tok_psT_free)))
            tok_elnb_read = tmax(*tls)
            tok_elb_read = tok_elnb_read
            tok_elmt = v_op(lambda e: e.tensor_copy(out=elmt[0:64, :], in_=psT[0:64, :]),
                            tmax(*tls, [tok_elmt_read]))
            tok_psT_free = [tok_elmt]
            lpv = []
            for lh in range(LHN):
                s_ = (lh % 2) * 4 + lh // 2
                r0 = (lh % 2) * 64
                dst = psA[r0:r0 + 64,
                          (lh % 2) * 512 + (lh // 2) * 128: (lh % 2) * 512 + (lh // 2 + 1) * 128]
                tp = (0, 64) if lh % 2 else None
                lpv.append(t_op(lambda e, lh=lh, s_=s_, dst=dst, tp=tp: e.matmul(
                    dst, gvtm[0:64, lh * 64:(lh + 1) * 64],
                    elmt[0:64, s_ * 128:(s_ + 1) * 128],
                    start=True, stop=True, tile_position=tp),
                    tmax([tok_elmt], [gates_tok["gv"]], tok_psA_free, [tok_expl])))
            tok_lpv = tmax(*lpv)
            tok_elmt_read = tok_lpv
            tok_rem1 = tmax(
                v_op(lambda e: e.tensor_copy(out=remfm[0:64, :], in_=psA[0:64, 0:512]),
                     tmax(tok_lpv, [tok_remfm_read])),
                v_op(lambda e: e.tensor_copy(out=remfm[64:128, :], in_=psA[64:128, 512:1024])))
            rom = []
            for c in range(4):
                rom.append(t_op(lambda e, c=c: e.matmul(
                    psA[:, 0:512], remfm[:, c * 128:(c + 1) * 128],
                    routb[:, c * 1024: c * 1024 + 512],
                    start=(c == 0), stop=(c == 3)),
                    tmax([tok_rem1], tok_rout)))
                rom.append(t_op(lambda e, c=c: e.matmul(
                    psB[:, 0:512], remfm[:, c * 128:(c + 1) * 128],
                    routb[:, c * 1024 + 512: (c + 1) * 1024],
                    start=(c == 0), stop=(c == 3)),
                    tmax(tok_psB_free)))
            tok_rom = tmax(*rom)
            tok_remfm_read = tok_rom

            tok_recr = v_op(lambda e: e.reciprocal(recr[:], psM[:, 0:16]),
                            tmax([tok_pv_last]))
            tok_psM_free = [tok_recr]
            tok_grc = v_op(lambda e: e.tensor_copy(out=gsig[:, 32:48], in_=gsig[:, 16:32]),
                           [tok_gsig])
            tok_sl_v = v_op(lambda e: e.tensor_tensor(gsig[:, 64:80], gsig[:, 0:16],
                                                      recr[:], ALU.mult))
            tok_lse = v_op(lambda e: e.tensor_copy(out=localsb[:], in_=psL[:, 0:1024]),
                           tmax([tok_psL_w], [tok_localsb_read]))
            tok_psL_free = [tok_lse]
            tok_re1 = v_op(lambda e: e.tensor_copy(out=remsb[:, 0:512], in_=psA[:, 0:512]),
                           tmax(tok_rom, [tok_remsb_read]))
            tok_re2 = v_op(lambda e: e.tensor_copy(out=remsb[:, 512:1024], in_=psB[:, 0:512]))
            tok_gt2 = t_op(lambda e: e.transpose(psM[0:96, 128:256], gsig[:, 0:96],
                                                 identf[:]),
                           tmax([tok_sl_v], [tok_grc], [tok_idf], tok_psM_free))
            tok_gte2 = v_op(lambda e: e.tensor_copy(out=gstb[0:96, :], in_=psM[0:96, 128:256]),
                            [tok_gt2])
            tok_psM_free = [tok_gte2, tok_recr]
            exm = []
            for half_i in range(2):
                exm.append(t_op(lambda e, half_i=half_i: e.matmul(
                    psA[:, half_i * 512:(half_i + 1) * 512],
                    gstb[32:48, :], exp16[32:48, half_i * 512:(half_i + 1) * 512],
                    start=True, stop=True),
                    tmax([tok_gte2], [tok_exp16], [tok_re1])))
                exm.append(t_op(lambda e, half_i=half_i: e.matmul(
                    psB[:, half_i * 512:(half_i + 1) * 512],
                    gstb[64:80, :], exp16[64:80, half_i * 512:(half_i + 1) * 512],
                    start=True, stop=True),
                    tmax([tok_gte2], [tok_re2])))
            tok_exm = tmax(*exm)
            tok_ga = v_op(lambda e, qt=qt: e.tensor_tensor(
                gated_b[qt % 3][:], remsb[:], psA[:, 0:1024], ALU.mult),
                tmax(tok_exm, [tok_gb_read[qt % 3]]))
            tok_gsig_r = [tok_ga, tok_gte2]
            tok_remsb_read = tok_ga
            tok_lscale = v_op(lambda e: e.tensor_tensor(
                localsb[:], localsb[:], psB[:, 0:1024], ALU.mult),
                tmax(tok_exm, [tok_lse]))
            tok_psA_free = [tok_ga]
            tok_psB_free = [tok_lscale]
            tok_stt_q[qt % 3] = g_op(lambda e, qt=qt: e.tensor_tensor(
                gated_b[qt % 3][:], gated_b[qt % 3][:], localsb[:], ALU.add),
                tmax([tok_lscale], [tok_ga]))
            tok_localsb_read = tok_stt_q[qt % 3]
        if sec >= 99:
            if nqt_run > 1:
                part2(nqt_run - 2)
            if nqt_run > 0:
                part2(nqt_run - 1)
        tok_prev_kv_read = tmax(*kv_reads)
        tok_loopbuf_read = bar()

    tap("xres_attn", xres[:, 0:4096], tmax([tok_ln2_done]))

    if stage < 3 or stage >= 10:
        tok_ln4_done = bar()
        tok_out = dma(out_d.rearrange("(t q) d -> q t d", q=P),
                      xres.ap().rearrange("q (t d) -> q t d", d=D),
                      waits=tmax(tok_ln4_done))
        final_syncd = [(f"syncd{i}", p.cnt[f"syncd{i}"]) for i in range(NDMASEM)]
        return _finish(nc, p, sems, final_syncd, dbg_outs)
    ffn_bar = bar()
    tern_state["bar"] = ffn_bar

    tok_w1_read = [ffn_bar] * 4
    tok_w2_read = [ffn_bar, ffn_bar]
    tok_psX_free = [tmax(tok_psA_free), tmax(tok_psA_free),
                    tmax(tok_psB_free), tmax(tok_psB_free)]
    tok_psY_free = [tmax(tok_psL_free), tmax(tok_psL_free)]
    tok_psTf_free = ffn_bar
    tok_psT_free2 = tmax(tok_psT_free)
    tok_h_read = [None, None]
    tok_xd_read = [None, None]
    tok_xl3_read = None
    tok_ftm_read = None
    tok_ln3tmp_read = None
    tok_ln4_done = None

    FDUMP = fdump
    ln3_toks = []
    for t in range(NT):
        tok_s3 = v_op(lambda e, t=t: e.tensor_reduce(
            smv[:, 36 + t:37 + t], xres[:, t * D:(t + 1) * D], AX.X, ALU.add),
            tmax([tok_ln2_done], ffn_bar))
        tok_q3 = s_op(lambda e, t=t: e.activation(
            FDUMP[:, 0:1024], xres[:, t * D:(t + 1) * D], AF.Square, bias=0.0, scale=1.0,
            accum_out=smv[:, 52 + t:53 + t]), tmax([tok_ln2_done], [tok_s3], ffn_bar))
        ln3_toks.append(tok_q3)
    tok_st3 = tmax(*ln3_toks)
    v_op(lambda e: e.tensor_scalar(smv[:, 36:52], smv[:, 36:52], -1.0 / D, None, ALU.mult),
         tok_st3)
    v_op(lambda e: e.tensor_scalar(smv[:, 52:68], smv[:, 52:68], 1.0 / D, None, ALU.mult))
    v_op(lambda e: e.tensor_tensor(smv[:, 68:84], smv[:, 36:52], smv[:, 36:52], ALU.mult))
    tok_v3 = v_op(lambda e: e.tensor_tensor(smv[:, 52:68], smv[:, 52:68],
                                            smv[:, 68:84], ALU.subtract))
    s_op(lambda e: e.activation(smv[:, 52:68], smv[:, 52:68], AF.Ln, bias=epsc[:], scale=1.0),
         [tok_v3])
    tok_r3 = s_op(lambda e: e.activation(smv[:, 52:68], smv[:, 52:68], AF.Exp,
                                         bias=0.0, scale=-0.5))

    def build_xl3(g):
        nonlocal tok_psT_free2, tok_ln3tmp_read
        xl3_toks = []
        prev_tp = tok_psT_free2
        xl3b = xln3fm[g % 2]
        for i in range(4):
            tile = g * 4 + i
            tok_ap = v_op(lambda e, t=tile: e.tensor_scalar(
                ln3tmp[:], xres[:, t * D:(t + 1) * D], smv[:, 36 + t:37 + t],
                smv[:, 52 + t:53 + t], ALU.add, ALU.mult),
                tmax([tok_r3], [tok_ln3tmp_read]))
            tps = []
            for dc in range(DC):
                tps.append(t_op(lambda e, dc=dc: e.transpose(
                    psT[:, dc * 128:(dc + 1) * 128], ln3tmp[:, dc * 128:(dc + 1) * 128],
                    identb[:]), tmax([tok_ap], prev_tp)))
            tok_ln3tmp_read = tmax(*tps)
            dst = xl3b.ap().rearrange("q (c j) -> q c j", j=512)[:, :, i * 128:(i + 1) * 128]
            tok_ev = v_op(lambda e, dst=dst: e.tensor_copy(
                out=dst, in_=psT.ap().rearrange("q (c j) -> q c j", j=128)),
                tmax(*tps, [tok_xl3_read]))
            prev_tp = tok_ev
            xl3_toks.append(tok_ev)
        tok_psT_free2 = tmax(*xl3_toks)
        return tok_psT_free2

    tok_xl3 = build_xl3(0)
    for g in range(NG):
        xl3b = xln3fm[g % 2]

        gelu_toks = []
        mm1_toks = []
        if g == 0:
            emit_tern(3)
        for fc in range(FC):
            if g == 0:
                emit_tern(1 if fc < 28 else 2)
            sl = fc % 4
            bs = fc % 4
            if g == 0:
                tok_w1 = tmax(tern_state["w1s"][fc])
            else:
                srcw = w1tern_d[:, fc * 1024:(fc + 1) * 1024]
                tok_w1 = dma(w1colb[bs][:], srcw,
                             waits=tmax([tok_w1_read[bs]], [tern_state["w1c"][fc]]))
            ps_ap = (psA if sl < 2 else psB)[:, (sl % 2) * 512:(sl % 2 + 1) * 512]
            mm = None
            for dc in range(DC):
                mm = t_op(lambda e, dc=dc, bs=bs, ps_ap=ps_ap, xl3b=xl3b: e.matmul(
                    ps_ap,
                    w1colb[bs][:, dc * 128:(dc + 1) * 128],
                    xl3b[:, dc * 512:(dc + 1) * 512],
                    start=(dc == 0), stop=(dc == DC - 1)),
                    tmax(tok_w1, [tok_xl3], [tok_psX_free[sl]]))
            tok_w1_read[bs] = tmax([mm], [tern_state["w1c"][fc]] if g == 0 else None)
            mm1_toks.append(mm)
            tok_g = s_op(lambda e, fc=fc, g=g, ps_ap=ps_ap: e.activation(
                hbuf[g % 2][:, fc * 512:(fc + 1) * 512], ps_ap,
                AF.Gelu, bias=b1c[:, fc:fc + 1], scale=0.5),
                tmax([mm], [tok_h_read[g % 2]], [tok_b1]))
            tok_psX_free[sl] = tok_g
            gelu_toks.append(tok_g)
        tok_xl3_read = tmax(*mm1_toks)

        tok_xl3_next = build_xl3(g + 1) if g + 1 < NG else None

        ln4_in_toks = []
        for dc in range(DC):
            sl = dc % 2
            py = dc % 2
            if g == 0:
                tok_w2 = tmax(tern_state["w2s"][dc])
            else:
                srcw = w2tern_d[:, dc * 4096:(dc + 1) * 4096]
                tok_w2 = dma(w2colb[sl][:, 0:4096], srcw,
                             waits=tmax([tok_w2_read[sl]], tern_state["w2c"][dc]))
            mm = None
            for fc in range(FC):
                mm = t_op(lambda e, fc=fc, sl=sl, py=py, g=g: e.matmul(
                    psL[:, py * 512:(py + 1) * 512], w2colb[sl][:, fc * 128:(fc + 1) * 128],
                    hbuf[g % 2][:, fc * 512:(fc + 1) * 512],
                    start=(fc == 0), stop=(fc == FC - 1)),
                    tmax(tok_w2, [gelu_toks[fc]], [tok_psY_free[py]]))
            tok_w2_read[sl] = tmax([mm], tern_state["w2c"].get(dc) if g == 0 else None)
            if g == 0:
                emit_tern(4)
            tok_xd = v_op(lambda e, dc=dc, py=py: e.tensor_scalar(
                xdelta[dc % 2][:], psL[:, py * 512:(py + 1) * 512], 0.5, b2c[:, dc:dc + 1],
                ALU.mult, ALU.add),
                tmax([mm], [tok_xd_read[dc % 2]], [tok_b2]))
            tok_psY_free[py] = tok_xd
            tps = []
            for i in range(4):
                tps.append(t_op(lambda e, i=i, dc=dc: e.transpose(
                    psM[:, i * 128:(i + 1) * 128], xdelta[dc % 2][:, i * 128:(i + 1) * 128],
                    identf[:]), tmax([tok_xd], tok_psTf_free, [tok_idf])))
            tok_xd_read[dc % 2] = tmax(*tps)
            dstf = ffn_tm.ap().rearrange("q (t d) -> q t d", d=D)[:, :, dc * 128:(dc + 1) * 128]
            tok_fe = v_op(lambda e, dstf=dstf: e.tensor_copy(
                out=dstf, in_=psM.ap().rearrange("q (t d) -> q t d", d=128)),
                tmax(*tps, [tok_ftm_read]))
            tok_psTf_free = [tok_fe]
            ln4_in_toks.append(tok_fe)
        tok_ftm = tmax(*ln4_in_toks)
        tok_h_read[g % 2] = tmax(tok_w2_read[0], tok_w2_read[1])
        if g == 0:
            assert tern_state["i"] == 64 and len(tern_state["w2c"]) == DC
            tok_h_read[1] = tmax(tok_h_read[1], tern_state["all"],
                                 *tern_state["stg_free"], *tern_state["sa_free"],
                                 *tern_state["sb_free"])

        ftm_reads = []
        for i in range(4):
            tile = g * 4 + i
            c0 = 151 + 4 * (i % 2)
            tok_s4 = v_op(lambda e, i=i, c0=c0: e.tensor_reduce(
                smv[:, c0:c0 + 1], ffn_tm[:, i * D:(i + 1) * D], AX.X, ALU.add), tmax(tok_ftm))
            tok_q4 = s_op(lambda e, i=i, c0=c0: e.activation(
                FDUMP[:, 0:1024], ffn_tm[:, i * D:(i + 1) * D], AF.Square, bias=0.0, scale=1.0,
                accum_out=smv[:, c0 + 1:c0 + 2]), tmax(tok_ftm, [tok_s4]))
            v_op(lambda e, c0=c0: e.tensor_scalar(
                smv[:, c0 + 2:c0 + 3], smv[:, c0:c0 + 1], -1.0 / D, None, ALU.mult), [tok_q4])
            v_op(lambda e, c0=c0: e.tensor_scalar(
                smv[:, c0 + 1:c0 + 2], smv[:, c0 + 1:c0 + 2], 1.0 / D, None, ALU.mult))
            v_op(lambda e, c0=c0: e.tensor_tensor(
                smv[:, 170:171], smv[:, c0 + 2:c0 + 3], smv[:, c0 + 2:c0 + 3], ALU.mult))
            tok_v4 = v_op(lambda e, c0=c0: e.tensor_tensor(
                smv[:, c0 + 1:c0 + 2], smv[:, c0 + 1:c0 + 2], smv[:, 170:171], ALU.subtract))
            s_op(lambda e, c0=c0: e.activation(
                smv[:, c0 + 3:c0 + 4], smv[:, c0 + 1:c0 + 2], AF.Ln, bias=epsc[:], scale=1.0),
                [tok_v4])
            tok_r4 = s_op(lambda e, c0=c0: e.activation(
                smv[:, c0 + 3:c0 + 4], smv[:, c0 + 3:c0 + 4], AF.Exp, bias=0.0, scale=-0.5))
            tok_ap4 = v_op(lambda e, i=i, c0=c0: e.tensor_scalar(
                ftmp[:], ffn_tm[:, i * D:(i + 1) * D], smv[:, c0 + 2:c0 + 3],
                smv[:, c0 + 3:c0 + 4], ALU.add, ALU.mult),
                tmax([tok_r4], [tok_ln4_done]))
            tok_ln4 = g_op(lambda e, t=tile: e.tensor_tensor(
                xres[:, t * D:(t + 1) * D], xres[:, t * D:(t + 1) * D], ftmp[:], ALU.add),
                [tok_ap4])
            ftm_reads.append(tok_ap4)
            tok_ln4_done = tok_ln4
        tok_ftm_read = tmax(*ftm_reads)
        tok_xl3 = tok_xl3_next

    tok_out = dma(out_d.rearrange("(t q) d -> q t d", q=P),
                  xres.ap().rearrange("q (t d) -> q t d", d=D),
                  waits=tmax([tok_ln4_done]))
    final_syncd = [(f"syncd{i}", p.cnt[f"syncd{i}"]) for i in range(NDMASEM)]

    return _finish(nc, p, sems, final_syncd, dbg_outs)


def _finish(nc, p, sems, final_syncd, dbg_outs):
    for c in CH_STREAM:
        sems[c] = nc.semaphore(c + "_sem").__enter__()

    with nc.Block() as block:
        def run(eng_name, eng):
            for (ws, fn, channel, inc) in p.ops[eng_name]:
                for (ch, v) in ws:
                    eng.wait_ge(sems[ch], v)
                ins = fn(eng)
                if inc:
                    ins.then_inc(sems[channel], inc)
            if eng_name == "sync":
                for (ch, v) in final_syncd:
                    if v > 0:
                        eng.wait_ge(sems[ch], v)

        @block.sync
        def _(sync):
            run("sync", sync)

        @block.tensor
        def _(tensor):
            run("tensor", tensor)

        @block.vector
        def _(vector):
            run("vector", vector)

        @block.scalar
        def _(scalar):
            run("scalar", scalar)

        @block.gpsimd
        def _(gpsimd):
            run("gpsimd", gpsimd)

    return nc, dbg_outs


_CACHE = {}


def _get_nc(debug=False, stage=3):
    key = ("nc", debug, stage)
    if key not in _CACHE:
        _CACHE[key] = build_nc(debug, stage)
    return _CACHE[key]


def _host_consts():
    if "consts" in _CACHE:
        return _CACHE["consts"]
    identb = np.eye(P, dtype=np.float32).astype(ml_dtypes.bfloat16)
    identf = np.eye(P, dtype=np.float32)
    onesb = np.ones((P, 1), np.float32).astype(ml_dtypes.bfloat16)
    onesf = np.ones((P, P), np.float32)
    poolv = np.full((P, 1), 1.0 / 128.0, np.float32).astype(ml_dtypes.bfloat16)
    pp, qq = np.meshgrid(np.arange(P), np.arange(P), indexing="ij")
    m_c0 = (qq <= pp - 1).astype(np.float32)
    m_c2 = (qq >= pp).astype(np.float32)
    zz = np.zeros((P, P), np.float32)
    def pairs(c0, c2):
        return np.concatenate([c0, c0, c2, c2], axis=1)
    generic = pairs(m_c0, m_c2)
    var0 = pairs(zz, m_c2)
    var1 = pairs(zz, m_c2)
    e16 = np.zeros((P, 1024), np.float32)
    for base in (0, 32, 64):
        for h in range(16):
            e16[base + h, 64 * h:64 * (h + 1)] = 1.0
    _CACHE["consts"] = dict(identb=identb, identf=identf, onesb=onesb, onesf=onesf,
                            poolv=poolv, generic=generic, var0=var0, var1=var1,
                            exp16=e16.astype(ml_dtypes.bfloat16))
    return _CACHE["consts"]


def _pack_rout(r):
    out = np.zeros((P, 4 * D), np.float32)
    rr = r.reshape(8, 64, D)
    for lh in range(8):
        out[(lh % 2) * 64:(lh % 2) * 64 + 64, (lh // 2) * D:(lh // 2 + 1) * D] = rr[lh]
    return out


def kernel(L0_attn_x, L0_attn_wq, L0_attn_wk, L0_attn_wv, L0_attn_wgq, L0_attn_wgk,
           L0_attn_wgv, L0_attn_r_out_w, L0_attn_gate_local_w, L0_attn_gate_remote_w,
           L0_attn_w_out, L0_ffn_W1, L0_ffn_b1, L0_ffn_W2, L0_ffn_b2, _debug=False,
           _stage=3):
    nc, dbg_outs = _get_nc(_debug, _stage)
    C = _host_consts()
    bf16 = ml_dtypes.bfloat16
    f32 = np.float32

    x = np.ascontiguousarray(L0_attn_x, dtype=f32)
    gwbd = np.zeros((D, 32), f32)
    for h in range(H):
        gwbd[h * 64:(h + 1) * 64, h] = L0_attn_gate_local_w[h]
        gwbd[h * 64:(h + 1) * 64, 16 + h] = L0_attn_gate_remote_w[h]
    w1t = np.ascontiguousarray(
        np.asarray(L0_ffn_W1, f32).T.reshape(8, 128, 32, 128)
        .transpose(1, 2, 0, 3).reshape(128, 32768))
    w2t = np.ascontiguousarray(
        np.asarray(L0_ffn_W2, f32).T.reshape(4, 8, 128, 8, 128)
        .transpose(2, 3, 0, 1, 4).reshape(128, 32768))

    def _colmajor(w, noc):
        w = np.asarray(w, f32)
        return np.ascontiguousarray(
            w.reshape(8, 128, noc, 128).transpose(1, 2, 0, 3).reshape(128, noc * 1024))
    b1r = np.ascontiguousarray(np.asarray(L0_ffn_b1, f32).reshape(FC, P).T)
    b2r = np.ascontiguousarray(np.asarray(L0_ffn_b2, f32).reshape(DC, P).T)

    shared = dict(
        wq=_colmajor(L0_attn_wq, 8), wk=_colmajor(L0_attn_wk, 8),
        wv=_colmajor(L0_attn_wv, 8), wgq=_colmajor(L0_attn_wgq, 4),
        wgk=_colmajor(L0_attn_wgk, 4), wgv=np.ascontiguousarray(L0_attn_wgv, f32),
        r_out=_pack_rout(np.asarray(L0_attn_r_out_w, f32)),
        w_out=np.ascontiguousarray(L0_attn_w_out, f32),
        gwbd=gwbd, w1t=w1t, w2t=w2t, b1r=b1r, b2r=b2r,
        identb=C["identb"], identf=C["identf"], onesb=C["onesb"], onesf=C["onesf"],
        poolv=C["poolv"], exp16=C["exp16"],
    )

    in_maps = []
    for k in range(8):
        b, q = k // 4, k % 4
        xin = np.zeros((T + 256, D), f32)
        if q > 0:
            xin[0:256] = x[b, q * T - 256: q * T]
        xin[256:] = x[b, q * T: (q + 1) * T]
        ones_pair = np.ones((P, 256), np.float32)
        if q == 0:
            maskt = np.concatenate([C["var0"], C["generic"], 0 * ones_pair], axis=1)
        else:
            maskt = np.concatenate([C["generic"], C["generic"], ones_pair], axis=1)
        lm = np.zeros((NT, P, 8, 64), f32)
        for i in range(NT):
            c = q * NT + i
            lm[i, :, :, 0:c + 1] = 1.0
        m = dict(shared)
        m["xin"] = xin
        m["maskt"] = maskt.astype(bf16)
        m["lmask"] = lm.reshape(NT * P, 512).astype(bf16)
        m["bsel"] = np.full((P, 1), float(b), f32)
        in_maps.append(m)

    res = run_bass_kernel_spmd(nc, in_maps, core_ids=list(range(8)))
    out = np.empty((2, 8192, D), f32)
    for k in range(8):
        b, q = k // 4, k % 4
        out[b, q * T:(q + 1) * T] = res.results[k]["out"]
    if _debug:
        return out, [ {n: r[n] for n in dbg_outs if n in r} for r in
                      [res.results[k] for k in range(8)] ]
    return out


# version 22
# speedup vs baseline: 1.0249x; 1.0249x over previous
import numpy as np
import ml_dtypes
import concourse.bass as bass
import concourse.mybir as mybir
from concourse.bass_utils import run_bass_kernel_spmd

F32 = mybir.dt.float32
BF16 = mybir.dt.bfloat16
AF = mybir.ActivationFunctionType
ALU = mybir.AluOpType
AX = mybir.AxisListType

P = 128
T = 2048
NT = 16
D = 1024
DC = 8
H = 16
LHN = 8
FF = 4096
FC = 32
NQT = 8
NHT = 10
GT = 512
NG = 4
EPS = 1e-5
SBUF_LIMIT = 229376

ENGINES = ("sync", "tensor", "vector", "scalar", "gpsimd")
NDMASEM = 8
CH_STREAM = {"tensor": "tensor", "vector": "vector", "scalar": "scalar",
             "pool": "gpsimd", "cc": "gpsimd"}
for _i in range(NDMASEM):
    CH_STREAM[f"syncd{_i}"] = "sync"
for _i in range(2):
    CH_STREAM[f"poold{_i}"] = "gpsimd"
COMPUTE_CH = ("tensor", "vector", "scalar", "pool")


class Plan:
    def __init__(self):
        self.ops = {e: [] for e in ENGINES}
        self.cnt = {c: 0 for c in CH_STREAM}
        self.waited = {e: {} for e in ENGINES}

    def emit(self, stream, fn, waits=(), channel=None, inc=0):
        waits = tmax(*waits)
        ws = []
        for (ch, v) in waits:
            if v <= 0:
                continue
            assert v <= self.cnt[ch], f"wait on future {ch}:{v} > {self.cnt[ch]}"
            if ch in COMPUTE_CH and CH_STREAM[ch] == stream and v < self.cnt[ch]:
                continue
            if self.waited[stream].get(ch, 0) >= v:
                continue
            self.waited[stream][ch] = v
            ws.append((ch, v))
        tok = None
        if inc:
            self.cnt[channel] += inc
            tok = (channel, self.cnt[channel])
        self.ops[stream].append((ws, fn, channel, inc))
        return tok


def tmax(*toks):
    out = {}

    def add(t):
        if t is None:
            return
        if isinstance(t, list):
            for it in t:
                add(it)
        else:
            c, v = t
            out[c] = max(out.get(c, 0), v)

    for t in toks:
        add(t)
    return list(out.items())


class Mem:
    def __init__(self, nc):
        self.nc = nc
        self.cur = int(nc.sbuf_base)
        self.limit = int(nc.sbuf_top)
        self.peak = 0

    def alloc(self, name, shape, dtype, at=None):
        per_part = int(np.prod(shape[1:])) * mybir.dt.size(dtype)
        addr = self.cur if at is None else at
        addr = (addr + 31) & ~31
        end = addr + per_part
        if at is None:
            self.cur = end
        self.peak = max(self.peak, end)
        assert end <= self.limit, f"SBUF overflow at {name}: {end} > {self.limit}"
        return self.nc.alloc_sbuf_tensor_at(name, list(shape), dtype, offset=addr)


def build_nc(debug=False, stage=3):
    nc = bass.Bass("TRN2", target_bir_lowering=False)
    p = Plan()
    mem = Mem(nc)
    dbg_outs = []

    def dram_in(name, shape, dtype=F32):
        return nc.dram_tensor(name, shape, dtype, kind="ExternalInput").ap()

    xin = dram_in("xin", [T + 256, D])
    wq_d = dram_in("wq", [P, DC * 1024]); wk_d = dram_in("wk", [P, DC * 1024])
    wv_d = dram_in("wv", [P, DC * 1024])
    wgq_d = dram_in("wgq", [P, 4 * 1024]); wgk_d = dram_in("wgk", [P, 4 * 1024])
    wgv_d = dram_in("wgv", [D, 512])
    rout_d = dram_in("r_out", [P, 4 * D])
    wout_d = dram_in("w_out", [D, D])
    gwbd_d = dram_in("gwbd", [D, 32])
    w1t_d = dram_in("w1t", [P, FC * 1024])
    w2t_d = dram_in("w2t", [P, FC * 1024])
    b1r_d = dram_in("b1r", [P, FC])
    b2r_d = dram_in("b2r", [P, DC])
    maskt_d = dram_in("maskt", [P, 1280], BF16)
    lmask_d = dram_in("lmask", [NT * P, 512], BF16)
    identb_d = dram_in("identb", [P, P], BF16)
    identf_d = dram_in("identf", [P, P], F32)
    onesb_d = dram_in("onesb", [P, 1], BF16)
    onesf_d = dram_in("onesf", [P, P], F32)
    poolv_d = dram_in("poolv", [P, 1], BF16)

    out_d = nc.dram_tensor("out", [T, D], F32, kind="ExternalOutput").ap()
    pool_src = nc.dram_tensor("pool_src", [D, 16], F32).ap()
    pool_dst = nc.dram_tensor("pool_dst", [8 * D, 16], F32, addr_space="Shared").ap()
    w1tern_d = nc.dram_tensor("w1tern", [P, FC * 1024], BF16).ap()
    xsave_d = nc.dram_tensor("xsave_d", [P, 2048], BF16).ap()
    w2tern_d = nc.dram_tensor("w2tern", [P, FC * 1024], BF16).ap()

    xres = mem.alloc("xres", [P, NT * D], F32)
    stg = mem.alloc("stg", [P, 2 * 1024], F32)
    identb = mem.alloc("identb", [P, P], BF16)
    identf = mem.alloc("identf", [P, P], F32)
    onesb = mem.alloc("onesb", [P, 1], BF16)
    onesf = mem.alloc("onesf", [P, P], F32)
    poolv = mem.alloc("poolv", [P, 1], BF16)
    smv = mem.alloc("smv", [P, 192], F32)
    epsc = mem.alloc("epsc", [P, 1], F32)
    b1c = mem.alloc("b1c", [P, FC], F32)
    b2c = mem.alloc("b2c", [P, DC], F32)
    tgA = mem.alloc("tgA", [P, 1024], BF16)
    tgB = mem.alloc("tgB", [P, 1024], BF16)
    UNION = mem.cur

    kfm = mem.alloc("kfm", [P, DC * 1280], BF16)
    vtm = mem.alloc("vtm", [P, NHT * D], BF16)
    qfm = mem.alloc("qfm", [P, DC * 1024], BF16)
    gqfm = mem.alloc("gqfm", [P, 4 * 1024], BF16)
    pooledb = mem.alloc("pooledb", [P, DC * 64], BF16)
    gkfm = mem.alloc("gkfm", [P, 4 * 64], BF16)
    gvtm = mem.alloc("gvtm", [P, 512], BF16)
    maskt = mem.alloc("maskt_sb", [P, 1280], BF16)

    gwbdb = mem.alloc("gwbdb", [P, 256], BF16)
    routb = mem.alloc("routb", [P, 4 * D], BF16)
    woutb = mem.alloc("woutb", [P, DC * D], BF16)
    wcolb = [mem.alloc(f"wcolb{i}", [P, 1024], BF16) for i in range(2)]
    expg = mem.alloc("expg", [P, 32], F32)
    gsig = mem.alloc("gsig", [P, 96], F32)
    recr = mem.alloc("recr", [P, 16], F32)
    rs8 = mem.alloc("rs8", [P, 16], F32)
    gstb = mem.alloc("gstb", [P, 128], BF16)
    OV = mem.cur
    xhalo = mem.alloc("xhalo", [P, 2 * D], F32)
    xlnfm = mem.alloc("xlnfm", [P, DC * 1280], BF16)
    xltmp = [mem.alloc("xltmp0", [P, D], BF16)] * 2
    vscr = [mem.alloc("vscr0", [P, 512], BF16)] * 2
    ATTN_END = mem.cur
    mem.cur = UNION
    pooled_g = [mem.alloc(f"pooled_g{i}", [P, 512], F32) for i in range(2)]
    pooled_f = mem.alloc("pooled_f", [P, 512], F32)
    pooled_own = mem.alloc("pooled_own", [P, 128], F32)
    bselb = mem.alloc("bselb", [P, 1], F32)
    mem.cur = OV
    etb = [mem.alloc(f"etb{i}", [P, 768], BF16) for i in range(2)]
    elb = mem.alloc("elb", [P, 512], BF16)
    elmt = mem.alloc("elmt", [P, 1024], BF16)
    remfm = mem.alloc("remfm", [P, 512], BF16)
    gated_b = [mem.alloc(f"gated_b{i}", [P, D], BF16) for i in range(3)]
    gatedt = mem.alloc("gatedt", [P, D], BF16)

    localsb = mem.alloc("localsb", [P, D], BF16)
    remsb = mem.alloc("remsb", [P, D], BF16)
    exp16 = mem.alloc("exp16", [P, 1024], BF16)
    lmaskw = mem.alloc("lmaskw", [P, 512], BF16)
    aotmp = [mem.alloc(f"aotmp{i}", [P, D], F32) for i in range(2)]
    assert mem.cur <= ATTN_END, f"loop overlay too big: {mem.cur} vs {ATTN_END}"



    mem.cur = UNION + 4096
    HB0 = (mem.cur + 31) & ~31
    hbuf = [mem.alloc(f"hbuf{i}", [P, FC * 512], BF16) for i in range(2)]
    HB1 = HB0 + FC * 512 * 2
    tstg = [mem.alloc(f"tstg{i}", [P, 1024], F32, at=HB1 + i * 4096) for i in range(4)]
    tsa = [mem.alloc(f"tsa{i}", [P, 1024], BF16, at=HB1 + 16384 + i * 2048) for i in range(2)]
    tsb = [mem.alloc(f"tsb{i}", [P, 1024], BF16, at=HB1 + 20480 + i * 2048) for i in range(2)]
    fdump = mem.alloc("fdump", [P, 1024], BF16, at=HB1 + 24576)
    xln3fm = [mem.alloc("xln3fm0", [P, DC * 512], BF16)] * 2
    w1colb = [mem.alloc(f"w1colb{i}", [P, 1024], BF16) for i in range(4)]
    w2colb = [mem.alloc(f"w2colb{i}", [P, FC * 128], BF16) for i in range(2)]
    xdelta = [mem.alloc(f"xdelta{i}", [P, 512], F32) for i in range(2)]
    ffn_tm = mem.alloc("ffn_tm", [P, 4 * D], F32)
    ln3tmp = mem.alloc("ln3tmp", [P, 1024], BF16)
    ftmp = mem.alloc("ftmp", [P, D], F32)
    print("FFN region end:", mem.cur, "attn end:", ATTN_END, "limit:", mem.limit)
    assert mem.cur <= max(ATTN_END, mem.cur) and mem.cur <= mem.limit

    psA = nc.alloc_psum_tensor("psA", [P, 1024], F32)
    psB = nc.alloc_psum_tensor("psB", [P, 1024], F32)
    psL = nc.alloc_psum_tensor("psL", [P, 1024], F32)
    psM = nc.alloc_psum_tensor("psM", [P, 512], F32)
    psT = nc.alloc_psum_tensor("psT", [P, 1024], BF16)

    sems = {}

    dma_rot = [0]

    def dma(dst, src, waits=()):
        k = dma_rot[0] % NDMASEM
        dma_rot[0] += 1
        ch = f"syncd{k}"
        prior = [(ch, p.cnt[ch])]
        return p.emit("sync", lambda e, d=dst, s=src: e.dma_start(out=d, in_=s),
                      tmax(list(waits), prior), ch, 16)

    pdma_rot = [0]

    def dma_p(dst, src, waits=()):
        k = pdma_rot[0] % 2
        pdma_rot[0] += 1
        ch = f"poold{k}"
        prior = [(ch, p.cnt[ch])]
        return p.emit("gpsimd", lambda e, d=dst, s=src: e.dma_start(out=d, in_=s),
                      tmax(list(waits), prior), ch, 16)

    def t_op(fn, waits=()):
        return p.emit("tensor", fn, waits, "tensor", 1)

    def _chained(stream, ch, fn, waits):
        prior = [(ch, p.cnt[ch])]
        return p.emit(stream, fn, tmax(list(waits), prior), ch, 1)

    def v_op(fn, waits=()):
        return _chained("vector", "vector", fn, waits)

    def s_op(fn, waits=()):
        return _chained("scalar", "scalar", fn, waits)

    def g_op(fn, waits=()):
        return _chained("gpsimd", "pool", fn, waits)

    def bar():
        chs = list(COMPUTE_CH) + [f"syncd{i}" for i in range(NDMASEM)] + \
            [f"poold{i}" for i in range(2)]
        return [(c, p.cnt[c]) for c in chs]

    def tap(name, ap, waits):
        if not debug:
            return None
        dd = nc.dram_tensor("tap_" + name, [int(ap.shape[0]), int(np.prod(ap.shape[1:]))],
                            ap.dtype, kind="ExternalOutput").ap()
        dbg_outs.append("tap_" + name)
        flat = ap
        return dma(dd, flat, waits)

    cast_busy = [None, None]
    cast_buf = [0]

    def _stg_view(b, src):
        n = int(np.prod(src.shape[1:]))
        assert n <= 1024, n
        npart = int(src.shape[0])
        flat = stg[0:npart, b * 1024: b * 1024 + n]
        if len(src.shape) == 3:
            shaped = flat.rearrange("q (a b) -> q a b", a=int(src.shape[1]))
        else:
            shaped = flat
        return flat, shaped

    def stream_cast(dst, src, extra_waits=None):
        b = cast_buf[0]
        cast_buf[0] ^= 1
        flat, shaped = _stg_view(b, src)
        tok_d = dma(shaped, src, waits=tmax(cast_busy[b], extra_waits))
        tok_c = v_op(lambda e, sv=flat, dd=dst: e.tensor_copy(out=dd, in_=sv), [tok_d])
        cast_busy[b] = tok_c
        return tok_c

    def stage_load(src, cols, extra_waits=None):
        b = cast_buf[0]
        cast_buf[0] ^= 1
        flat, shaped = _stg_view(b, src)
        tok_d = dma(shaped, src, waits=tmax(cast_busy[b], extra_waits))
        return b, flat, tok_d

    tok_eps = v_op(lambda e: e.memset(epsc[:], EPS))
    tok_idb = dma(identb[:], identb_d)
    tok_idf = dma(identf[:], identf_d)
    tok_onb = dma(onesb[:], onesb_d)
    tok_onf = dma(onesf[:], onesf_d)
    tok_pv = dma(poolv[:], poolv_d)
    tok_b1 = dma(b1c[:], b1r_d)
    tok_b2 = dma(b2c[:], b2r_d)
    tok_bs = dma(bselb[:], dram_in("bsel", [P, 1]))
    tok_mask = dma(maskt[:], maskt_d)
    exp16_d = dram_in("exp16", [P, 1024], BF16)
    tok_xg = [dma(xres.ap().rearrange("q (t d) -> q t d", d=D)[:, 4 * i:4 * (i + 1), :],
                  xin[256 + 4 * i * P:256 + 4 * (i + 1) * P, :]
                  .rearrange("(t q) d -> q t d", q=P)) for i in range(4)]
    tok_x = tmax(*tok_xg)
    tok_xh = dma(xhalo.ap().rearrange("q (t d) -> q t d", d=D),
                 xin[0:256, :].rearrange("(t q) d -> q t d", q=P))

    tok_gwbd = stream_cast(gwbdb[:], gwbd_d.rearrange("(c q) g -> q c g", q=P))
    tok_rout = tmax(*[stream_cast(routb[:, i * 1024:(i + 1) * 1024],
                                  rout_d[:, i * 1024:(i + 1) * 1024])
                      for i in range(4)])
    tok_wout = tmax(*[stream_cast(woutb[:, i * 1024:(i + 1) * 1024],
                                  wout_d.rearrange("(c q) d -> q c d", q=P)[:, i:i + 1, :])
                      for i in range(8)])

    tok_z = v_op(lambda e: e.memset(smv[:, 114:116], 0.0))
    v_op(lambda e: e.memset(gsig[:], 0.0))
    abs_state = {"i": 0, "prev": {0: tok_z, 1: tok_z}, "toks": [],
                 "pbar": None, "pfree": [None, None]}
    tern_state = {"i": 0, "w1c": [None] * FC, "w2c": {}, "w1s": [None] * FC, "w2s": {},
                  "thr": None, "all": [],
                  "stg_free": [None] * 4, "sa_free": [None, None], "sb_free": [None, None],
                  "bar": None}

    def emit_tern(n):
        if tern_state["thr"] is None:
            return
        while n > 0 and tern_state["i"] < 64:
            i = tern_state["i"]
            tern_state["i"] += 1
            n -= 1
            ss = i % 4
            sp = i % 2
            tok_thr2 = tern_state["thr"]
            if i < 32:
                fc = i
                src_ap = w1t_d[:, fc * 1024:(fc + 1) * 1024]
                dst_ap = w1tern_d[:, fc * 1024:(fc + 1) * 1024]
                b0, b1 = smv[:, 146:147], smv[:, 147:148]
                wdst = w1colb[fc % 4][:]
                wread = tok_w1_read[fc % 4]
            else:
                j = i - 32
                dc, hh = j // 4, j % 4
                src_ap = w2t_d[:, (dc * 4 + hh) * 1024:(dc * 4 + hh + 1) * 1024]
                dst_ap = w2tern_d[:, (dc * 4 + hh) * 1024:(dc * 4 + hh + 1) * 1024]
                b0, b1 = smv[:, 148:149], smv[:, 149:150]
                wdst = w2colb[dc % 2][:, hh * 1024:(hh + 1) * 1024]
                wread = tok_w2_read[dc % 2]
            tok_d = dma(tstg[ss][:], src_ap,
                        waits=tmax(tern_state["stg_free"][ss], tok_thr2, tern_state["bar"]))
            tok_sa = s_op(lambda e, ss=ss, sp=sp, b0=b0: e.activation(
                tsa[sp][:], tstg[ss][:], AF.Sign, bias=b0, scale=1.0),
                tmax([tok_d], tok_thr2, tern_state["sa_free"][sp]))
            tok_sb = s_op(lambda e, ss=ss, sp=sp, b1=b1: e.activation(
                tsb[sp][:], tstg[ss][:], AF.Sign, bias=b1, scale=1.0),
                tmax(tern_state["sb_free"][sp]))
            tern_state["stg_free"][ss] = [tok_sb]
            tok_add = v_op(lambda e, sp=sp, wdst=wdst: e.tensor_tensor(
                wdst, tsa[sp][:], tsb[sp][:], ALU.add), tmax([tok_sb], wread))
            tern_state["sa_free"][sp] = [tok_add]
            tern_state["sb_free"][sp] = [tok_add]
            tok_cw = dma_p(dst_ap, wdst, waits=[tok_add])
            tern_state["all"].append(tok_cw)
            if i < 32:
                tern_state["w1c"][i] = tok_cw
                tern_state["w1s"][i] = [tok_add]
            else:
                j = i - 32
                tern_state["w2c"].setdefault(j // 4, []).append(tok_cw)
                tern_state["w2s"].setdefault(j // 4, []).append(tok_add)

    def emit_thresholds():
        nonlocal tok_psL_free
        tok_abs = tmax(*abs_state["toks"])
        assert abs_state["i"] == 64
        thm = t_op(lambda e: e.matmul(psL[:, 0:2], onesf[:], smv[:, 114:116],
                                      start=True, stop=True),
                   tmax(tok_abs, tok_psL_free, [tok_onf]))
        tok_thv = v_op(lambda e: e.tensor_scalar(smv[:, 164:166], psL[:, 0:2],
                                                 1.0 / (D * FF), None, ALU.mult), [thm])
        tok_psL_free = [tok_thv]
        v_op(lambda e: e.tensor_scalar(smv[:, 164:166], smv[:, 164:166], 1e-8, None, ALU.max))
        v_op(lambda e: e.tensor_scalar(smv[:, 147:148], smv[:, 164:165], 0.5, None, ALU.mult))
        v_op(lambda e: e.tensor_scalar(smv[:, 149:150], smv[:, 165:166], 0.5, None, ALU.mult))
        v_op(lambda e: e.tensor_scalar(smv[:, 146:147], smv[:, 164:165], -0.5, None, ALU.mult))
        tok_t = v_op(lambda e: e.tensor_scalar(smv[:, 148:149], smv[:, 165:166],
                                               -0.5, None, ALU.mult))
        tern_state["thr"] = [tok_t]
        return tok_thv

    def emit_abs(n):
        while n > 0 and abs_state["i"] < 64:
            i = abs_state["i"]
            abs_state["i"] += 1
            n -= 1
            if i < 32:
                src_ap = w1t_d[:, i * 1024:(i + 1) * 1024]
                pc = 114
            else:
                j = i - 32
                src_ap = w2t_d[:, j * 1024:(j + 1) * 1024]
                pc = 115
            scr = 116 + (i % 2)
            if abs_state["pbar"] is None:
                b, stg_v, tok_d = stage_load(src_ap, 1024)
            else:
                b = None
                stg_v = xhalo[:, (i % 2) * 1024:(i % 2 + 1) * 1024]
                tok_d = dma_p(stg_v, src_ap,
                              waits=tmax(abs_state["pfree"][i % 2], abs_state["pbar"]))
            tok_a = s_op(lambda e, sv=stg_v, scr=scr: e.activation(
                tgA[:, 0:1024], sv, AF.Abs,
                bias=0.0, scale=1.0, accum_out=smv[:, scr:scr + 1]),
                tmax([tok_d], [abs_state["prev"][i % 2]]))
            tok_add = v_op(lambda e, pc=pc, scr=scr: e.tensor_tensor(
                smv[:, pc:pc + 1], smv[:, pc:pc + 1], smv[:, scr:scr + 1], ALU.add),
                [tok_a])
            if b is not None:
                cast_busy[b] = tok_a
            else:
                abs_state["pfree"][i % 2] = [tok_a]
            abs_state["prev"][i % 2] = tok_add
            abs_state["toks"].append(tok_add)


    def xtile(i):
        return xres[:, i * D:(i + 1) * D] if i < NT else xhalo[:, (i - NT) * D:(i - NT + 1) * D]

    red_toks = []
    for i in range(18):
        w = [tok_xg[i // 4]] if i < NT else [tok_xh]
        red_toks.append(v_op(lambda e, i=i: e.tensor_reduce(
            smv[:, 36 + i:37 + i], xtile(i), AX.X, ALU.add), w))
        red_toks.append(s_op(lambda e, i=i: e.activation(
            tgA[:, 0:D], xtile(i), AF.Square, bias=0.0, scale=1.0,
            accum_out=smv[:, 54 + i:55 + i]), w))
    tok_red = tmax(*red_toks)
    v_op(lambda e: e.tensor_scalar(smv[:, 0:18], smv[:, 36:54], -1.0 / D, None, ALU.mult), tok_red)
    v_op(lambda e: e.tensor_scalar(smv[:, 54:72], smv[:, 54:72], 1.0 / D, None, ALU.mult), tok_red)
    v_op(lambda e: e.tensor_tensor(smv[:, 72:90], smv[:, 0:18], smv[:, 0:18], ALU.mult))
    tok_var = v_op(lambda e: e.tensor_tensor(smv[:, 54:72], smv[:, 54:72], smv[:, 72:90], ALU.subtract))
    s_op(lambda e: e.activation(smv[:, 18:36], smv[:, 54:72], AF.Ln, bias=epsc[:], scale=1.0), [tok_var])
    tok_stats = s_op(lambda e: e.activation(smv[:, 18:36], smv[:, 18:36], AF.Exp, bias=0.0, scale=-0.5))

    def ln1_apply(dst, i, waits):
        return v_op(lambda e, i=i, dst=dst: e.tensor_scalar(
            dst, xtile(i), smv[:, i:i + 1], smv[:, 18 + i:19 + i], ALU.add, ALU.mult), waits)

    pool_toks = []
    prev_pool = None
    for i in range(NT):
        tok_ap = ln1_apply(xltmp[i % 2][:], i, tmax([tok_stats], prev_pool))
        mm = []
        for dc in range(DC):
            mm.append(t_op(lambda e, i=i, dc=dc: e.matmul(
                psM[:, 64 + dc * 16 + i: 65 + dc * 16 + i],
                xltmp[i % 2][:, dc * 128:(dc + 1) * 128], poolv[:],
                start=True, stop=True), tmax([tok_ap], [tok_pv])))
        prev_pool = tmax(*mm)
        pool_toks.append(prev_pool)
        emit_abs(1)
    tok_poole = v_op(lambda e: e.tensor_copy(out=pooled_own[:], in_=psM[:, 64:192]),
                     tmax(*pool_toks))
    tok_ps = dma(pool_src.rearrange("(c q) n -> q c n", q=P),
                 pooled_own.ap().rearrange("q (c n) -> q c n", n=16), waits=[tok_poole])
    tok_cc = p.emit("gpsimd", lambda e: e.collective_compute(
        "AllGather", ALU.bypass, replica_groups=[[0, 1, 2, 3, 4, 5, 6, 7]],
        ins=[pool_src[:, :]], outs=[pool_dst[:, :]]), [tok_ps], "cc", 1)
    wcol_use = [None, None]
    gates_tok = {}

    def emit_gates_weights():
        nonlocal tok_psA_free, tok_psM_free
        tok_pga = tmax(*[dma(
            pooled_g[b].ap().rearrange("q (c n) -> q c n", c=8)[:, :, r * 16:(r + 1) * 16],
            pool_dst[(4 * b + r) * 1024:(4 * b + r + 1) * 1024, :]
            .rearrange("(c q) n -> q c n", q=P),
            waits=[tok_cc]) for r in range(4) for b in range(2)])
        v_op(lambda e: e.tensor_tensor(pooled_f[:], pooled_g[1][:], pooled_g[0][:], ALU.subtract),
             tmax(tok_pga, [tok_bs]))
        tok_psel = v_op(lambda e: e.scalar_tensor_tensor(
            pooled_f[:], pooled_f[:], bselb[:], pooled_g[0][:], ALU.mult, ALU.add))
        tok_poolf = g_op(lambda e: e.tensor_copy(out=pooledb[:], in_=pooled_f[:]), [tok_psel])
        tap("pooledb", pooledb[:], [tok_poolf])

        tok_gk_mm = []
        for gc in range(4):
            tok_w = stream_cast(wcolb[gc % 2][:],
                                wgk_d[:, gc * 1024:(gc + 1) * 1024],
                                extra_waits=wcol_use[gc % 2])
            mm = None
            for dc in range(DC):
                mm = t_op(lambda e, gc=gc, dc=dc: e.matmul(
                    psM[:, 256 + gc * 64: 256 + (gc + 1) * 64],
                    wcolb[gc % 2][:, dc * 128:(dc + 1) * 128],
                    pooledb[:, dc * 64:(dc + 1) * 64],
                    start=(dc == 0), stop=(dc == DC - 1)),
                    tmax([tok_w], [tok_poolf]))
            wcol_use[gc % 2] = mm
            tok_gk_mm.append(mm)
        tok_gke = s_op(lambda e: e.activation(gkfm[:], psM[:, 256:512], AF.Copy, bias=0.0, scale=1.0),
                       tmax(*tok_gk_mm))
        tap("gkfm", gkfm[:], [tok_gke])

        tok_gv_mm = None
        for dc in range(DC):
            tok_w = stream_cast(wcolb[dc % 2][:, 0:512],
                                wgv_d.rearrange("(c q) g -> q c g", q=P)[:, dc:dc + 1, :],
                                extra_waits=wcol_use[dc % 2])
            tok_gv_mm = t_op(lambda e, dc=dc: e.matmul(
                psA[0:64, 0:512], pooledb[:, dc * 64:(dc + 1) * 64],
                wcolb[dc % 2][:, 0:512],
                start=(dc == 0), stop=(dc == DC - 1)),
                tmax([tok_w], [tok_poolf], [tok_gke], tok_psA_free))
            wcol_use[dc % 2] = tok_gv_mm
        tok_gv = v_op(lambda e: e.tensor_copy(out=gvtm[0:64, :], in_=psA[0:64, 0:512]), [tok_gv_mm])
        tap("gvtm", gvtm[0:64, :], [tok_gv])

        gates_tok["poolf"] = tok_poolf
        gates_tok["gke"] = tok_gke
        gates_tok["gv"] = tok_gv
        tok_psM_free = [tok_gke, tok_poole]
        tok_psA_free = [tok_gv]

    tok_psA_free = [tok_poole]
    tok_psB_free = [None]
    tok_psL_free = [None]
    tok_psM_free = [tok_poole]
    tok_psT_free = [None]
    tok_exp_sl = [None, None]
    tok_mask_sl = [None, None]
    tok_pv_sl = [None, None]
    tok_gb_read = [None, None, None]
    tok_elb_read = None
    tok_elnb_read = None
    tok_elmt_read = None
    tok_remfm_read = None
    tok_gatedt_read = None
    tok_expg_w = None
    tok_gsig_r = None
    tok_prev_kv_read = None
    tok_loopbuf_read = bar()
    tok_ln2_done = None
    tok_xsave = None
    tok_localsb_read = None
    tok_remsb_read = None
    tok_ao_read = [None, None]
    tok_stt_q = [None, None, None]
    tok_lmw_read = None

    if stage >= 10:
        n_halves = 1
    else:
        n_halves = 2 if stage >= 2 else (1 if stage >= 1 else 0)
    for half in range(n_halves):
        xln_toks = []
        prev_tp = tmax(tok_psT_free)
        j0 = 0
        if half == 1:
            tok_hcp = dma(xlnfm.ap().rearrange("q (c j) -> q c j", j=1280)[:, :, 0:256],
                          xsave_d.rearrange("q (c j) -> q c j", j=256),
                          waits=tmax(tok_loopbuf_read, tok_prev_kv_read, [tok_xsave]))
            xln_toks.append(tok_hcp)
            j0 = 2
        for j in range(j0, NHT):
            gi = half * NQT + j - 2
            si = gi if gi >= 0 else NT + (gi + 2)
            tok_ap = ln1_apply(xltmp[j % 2][:], si,
                               tmax([tok_stats], prev_tp, tok_loopbuf_read, tok_prev_kv_read))
            tps = []
            for dc in range(DC):
                tps.append(t_op(lambda e, j=j, dc=dc: e.transpose(
                    psT[:, dc * 128:(dc + 1) * 128],
                    xltmp[j % 2][:, dc * 128:(dc + 1) * 128], identb[:]),
                    tmax([tok_ap], prev_tp, [tok_idb])))
            dst = xlnfm.ap().rearrange("q (c j) -> q c j", j=1280)[:, :, j * 128:(j + 1) * 128]
            tok_ev = v_op(lambda e, dst=dst: e.tensor_copy(
                out=dst, in_=psT.ap().rearrange("q (c j) -> q c j", j=128)), tmax(*tps))
            prev_tp = tok_ev
            xln_toks.append(tok_ev)
        tok_xln = tmax(*xln_toks)
        tok_psT_free = [tok_xln]
        if half == 0:
            tok_xsave = dma(xsave_d.rearrange("q (c j) -> q c j", j=256),
                            xlnfm.ap().rearrange("q (c j) -> q c j", j=1280)[:, :, 1024:1280],
                            waits=tmax(tok_xln))
            tok_xln = tmax(tok_xln, [tok_xsave])
            abs_state["pbar"] = tmax(tok_xln)
            emit_gates_weights()
            tap("xlnfm", xlnfm[:], tmax([tok_xln]))

        def proj(w_dram, n_outc, span, scale, dst_fn, own_only, evac_extra=None):
            nonlocal tok_psA_free, tok_psB_free
            toks = []
            rot = [0]
            for oc in range(n_outc):
                tok_w = stream_cast(wcolb[oc % 2][:],
                                    w_dram[:, oc * 1024:(oc + 1) * 1024],
                                    extra_waits=wcol_use[oc % 2])
                ncs = [(0, 512), (512, 512), (1024, 256)] if span == 1280 else [(0, 512), (512, 512)]
                for (c0, nn) in ncs:
                    use_a = (rot[0] % 2 == 0)
                    rot[0] += 1
                    ps = psA if use_a else psB
                    ps_free = tok_psA_free if use_a else tok_psB_free
                    mm = None
                    off = 256 if own_only else 0
                    for dc in range(DC):
                        mm = t_op(lambda e, oc=oc, c0=c0, nn=nn, dc=dc, ps=ps, off=off: e.matmul(
                            ps[:, 0:nn],
                            wcolb[oc % 2][:, dc * 128:(dc + 1) * 128],
                            xlnfm[:, dc * 1280 + off + c0: dc * 1280 + off + c0 + nn],
                            start=(dc == 0), stop=(dc == DC - 1)),
                            tmax([tok_w], [tok_xln], ps_free))
                    wcol_use[oc % 2] = mm
                    tok_e = s_op(lambda e, oc=oc, c0=c0, nn=nn, ps=ps, sc=scale: e.activation(
                        dst_fn(oc, c0, nn), ps[:, 0:nn], AF.Copy, bias=0.0, scale=sc),
                        tmax([mm], evac_extra))
                    if use_a:
                        tok_psA_free = [tok_e]
                    else:
                        tok_psB_free = [tok_e]
                    toks.append(tok_e)
                emit_abs(2)
            return tmax(*toks)

        tok_k = proj(wk_d, DC, 1280, 1.0,
                     lambda oc, c0, nn: kfm[:, oc * 1280 + c0: oc * 1280 + c0 + nn], False,
                     evac_extra=[gates_tok["poolf"]] if half == 0 else None)
        tok_q = proj(wq_d, DC, 1024, 0.125,
                     lambda oc, c0, nn: qfm[:, oc * 1024 + c0: oc * 1024 + c0 + nn], True)
        tok_gq = proj(wgq_d, 4, 1024, 0.125,
                      lambda oc, c0, nn: gqfm[:, oc * 1024 + c0: oc * 1024 + c0 + nn], True)

        v_toks = []
        tok_vscr_read = [None, None]
        for oc in range(DC):
            tok_w = stream_cast(wcolb[oc % 2][:],
                                wv_d[:, oc * 1024:(oc + 1) * 1024],
                                extra_waits=wcol_use[oc % 2])
            for ci, (c0, nn) in enumerate([(0, 512), (512, 512), (1024, 256)]):
                use_a = (ci % 2 == 0)
                ps = psA if use_a else psB
                ps_free = tok_psA_free if use_a else tok_psB_free
                mm = None
                for dc in range(DC):
                    mm = t_op(lambda e, oc=oc, c0=c0, nn=nn, dc=dc, ps=ps: e.matmul(
                        ps[:, 0:nn],
                        wcolb[oc % 2][:, dc * 128:(dc + 1) * 128],
                        xlnfm[:, dc * 1280 + c0: dc * 1280 + c0 + nn],
                        start=(dc == 0), stop=(dc == DC - 1)),
                        tmax([tok_w], [tok_xln], ps_free))
                wcol_use[oc % 2] = mm
                sv = ci % 2
                tok_e = s_op(lambda e, nn=nn, ps=ps, sv=sv: e.activation(
                    vscr[sv][:, 0:nn], ps[:, 0:nn], AF.Copy, bias=0.0, scale=1.0),
                    tmax([mm], tok_vscr_read[sv]))
                if use_a:
                    tok_psA_free = [tok_e]
                else:
                    tok_psB_free = [tok_e]
                nblk = nn // 128
                tps = []
                for i in range(nblk):
                    tps.append(t_op(lambda e, i=i, sv=sv: e.transpose(
                        psT[:, i * 128:(i + 1) * 128], vscr[sv][:, i * 128:(i + 1) * 128],
                        identb[:]), tmax([tok_e], tok_psT_free)))
                tok_vscr_read[sv] = tmax(*tps)
                v0 = c0 // 128
                dstv = vtm.ap().rearrange("q (v d) -> q v d", d=D)[:, v0:v0 + nblk,
                                                                  oc * 128:(oc + 1) * 128]
                srcv = psT.ap().rearrange("q (v d) -> q v d", d=128)[:, 0:nblk, :]
                tok_vv = v_op(lambda e, dstv=dstv, srcv=srcv: e.tensor_copy(out=dstv, in_=srcv),
                              tmax(*tps))
                tok_psT_free = [tok_vv]
                v_toks.append(tok_vv)
            emit_abs(2)
        tok_v = tmax(*v_toks)
        if half == 0:
            tap("kfm", kfm[:, 0:2048], tmax([tok_k]))
            tap("qfm", qfm[:, 0:2048], tmax([tok_q]))
            tap("gqfm", gqfm[:, 0:2048], tmax([tok_gq]))
            tap("vtm", vtm[:, 0:2048], tmax([tok_v]))

        tok_projs = tmax([tok_k], [tok_q], [tok_gq], [tok_v])
        if half == 0:
            emit_abs(64)
            emit_thresholds()
        kv_reads = []
        tok_exp16 = dma(exp16[:], exp16_d, waits=tmax(tok_loopbuf_read, tok_projs))

        if 100 <= stage < 200:
            nqt_run, sec = 1, stage - 100
        elif stage >= 200:
            nqt_run, sec = 1, 2
        else:
            nqt_run, sec = (NQT if stage < 10 else min(stage - 10, NQT)), 99

        def part2(qt):
            nonlocal tok_psT_free, tok_gatedt_read, tok_psL_free, tok_ln2_done
            g_tile = half * NQT + qt
            gts = []
            for dc in range(DC):
                gts.append(t_op(lambda e, dc=dc, qt=qt: e.transpose(
                    psT[:, dc * 128:(dc + 1) * 128], gated_b[qt % 3][:, dc * 128:(dc + 1) * 128],
                    identb[:]), tmax(tok_stt_q[qt % 3], tok_psT_free)))
            tok_gb_read[qt % 3] = tmax(*gts)
            tok_gte = s_op(lambda e: e.activation(gatedt[:], psT[:, 0:1024], AF.Copy,
                                                  bias=0.0, scale=1.0),
                           tmax(*gts, [tok_gatedt_read]))
            tok_psT_free = [tok_gte]
            wom = []
            for dc in range(DC):
                wom.append(t_op(lambda e, dc=dc: e.matmul(
                    psL[:, 0:512], gatedt[:, dc * 128:(dc + 1) * 128],
                    woutb[:, dc * 1024: dc * 1024 + 512],
                    start=(dc == 0), stop=(dc == DC - 1)),
                    tmax([tok_gte], tok_psL_free, tok_wout)))
                wom.append(t_op(lambda e, dc=dc: e.matmul(
                    psL[:, 512:1024], gatedt[:, dc * 128:(dc + 1) * 128],
                    woutb[:, dc * 1024 + 512: (dc + 1) * 1024],
                    start=(dc == 0), stop=(dc == DC - 1))))
            tok_wo = tmax(*wom)
            tok_gatedt_read = tok_wo
            c0 = 90 + 4 * (qt % 2)
            ao = aotmp[qt % 2]
            tok_aoe = s_op(lambda e, ao=ao: e.activation(
                ao[:], psL[:, 0:1024], AF.Copy, bias=0.0, scale=1.0),
                tmax(tok_wo, [tok_ao_read[qt % 2]]))
            tok_psL_free = [tok_aoe]
            tok_s2 = v_op(lambda e, c0=c0, ao=ao: e.tensor_reduce(
                smv[:, c0:c0 + 1], ao[:], AX.X, ALU.add), [tok_aoe])
            tok_q2 = s_op(lambda e, c0=c0, ao=ao: e.activation(
                gatedt[:, 0:1024], ao[:], AF.Square, bias=0.0, scale=1.0,
                accum_out=smv[:, c0 + 1:c0 + 2]), tmax([tok_aoe], [tok_s2]))
            v_op(lambda e, c0=c0: e.tensor_scalar(
                smv[:, c0 + 2:c0 + 3], smv[:, c0:c0 + 1], -1.0 / D, None, ALU.mult), [tok_q2])
            v_op(lambda e, c0=c0: e.tensor_scalar(
                smv[:, c0 + 1:c0 + 2], smv[:, c0 + 1:c0 + 2], 1.0 / D, None, ALU.mult))
            v_op(lambda e, c0=c0: e.tensor_tensor(
                smv[:, 159:160], smv[:, c0 + 2:c0 + 3], smv[:, c0 + 2:c0 + 3], ALU.mult))
            tok_v2 = v_op(lambda e, c0=c0: e.tensor_tensor(
                smv[:, c0 + 1:c0 + 2], smv[:, c0 + 1:c0 + 2], smv[:, 159:160], ALU.subtract))
            s_op(lambda e, c0=c0: e.activation(
                smv[:, c0 + 3:c0 + 4], smv[:, c0 + 1:c0 + 2], AF.Ln, bias=epsc[:], scale=1.0),
                [tok_v2])
            tok_r2 = s_op(lambda e, c0=c0: e.activation(
                smv[:, c0 + 3:c0 + 4], smv[:, c0 + 3:c0 + 4], AF.Exp, bias=0.0, scale=-0.5))
            tok_app = v_op(lambda e, c0=c0, ao=ao: e.tensor_scalar(
                ao[:], ao[:], smv[:, c0 + 2:c0 + 3], smv[:, c0 + 3:c0 + 4],
                ALU.add, ALU.mult), [tok_r2])
            tok_ln2 = g_op(lambda e, g=g_tile, ao=ao: e.tensor_tensor(
                xres[:, g * D:(g + 1) * D], xres[:, g * D:(g + 1) * D], ao[:], ALU.add),
                [tok_app])
            tok_ao_read[qt % 2] = tok_ln2
            tok_ln2_done = tok_ln2

        def emit_pair(hp, qt, g_tile, mvar, with_pv=True):
            sl = hp % 2
            ps = psA if sl == 0 else psB
            ps_free = tok_psA_free if sl == 0 else tok_psB_free
            smm = []
            for hl in range(2):
                for c in range(3):
                    smm.append(t_op(lambda e, hp=hp, hl=hl, c=c, qt=qt, ps=ps: e.matmul(
                        ps[:, hl * 512 + c * 128: hl * 512 + (c + 1) * 128],
                        kfm[hl * 64: hl * 64 + 64,
                            hp * 1280 + (qt + c) * 128: hp * 1280 + (qt + c + 1) * 128],
                        qfm[hl * 64: hl * 64 + 64,
                            hp * 1024 + qt * 128: hp * 1024 + (qt + 1) * 128],
                        start=True, stop=True),
                        tmax([tok_projs], ps_free, [tok_exp_sl[sl]])))
            tok_sc = tmax(*smm)
            kv_reads.append(tok_sc)
            tok_exp = tmax(
                s_op(lambda e, sl=sl, ps=ps: e.activation(
                    etb[sl].ap().rearrange("q (c z) -> q c z", z=128)[:, 0:6:2, :],
                    ps[:, 0:384], AF.Exp, bias=0.0, scale=1.0),
                    tmax(tok_sc, [tok_mask_sl[sl]], [tok_pv_sl[sl]])),
                s_op(lambda e, sl=sl, ps=ps: e.activation(
                    etb[sl].ap().rearrange("q (c z) -> q c z", z=128)[:, 1:6:2, :],
                    ps[:, 512:896], AF.Exp, bias=0.0, scale=1.0)))
            tok_exp_sl[sl] = tok_exp
            if sl == 0:
                tok_psA_free2[0] = [tok_exp]
            else:
                tok_psB_free2[0] = [tok_exp]
            ev = etb[sl].ap().rearrange("q (c z) -> q c z", z=256)
            mks = [
                v_op(lambda e, ev=ev, mv=mvar: e.tensor_tensor(
                    ev[:, 0:3:2, :], ev[:, 0:3:2, :],
                    maskt.ap().rearrange("q (c z) -> q c z", z=256)[:, 2 * mv:2 * mv + 2, :],
                    ALU.mult),
                    tmax([tok_exp], [tok_mask])),
            ]
            if g_tile == 0:
                mks.append(v_op(lambda e, sl=sl: e.tensor_tensor(
                    etb[sl][:, 256:512], etb[sl][:, 256:512],
                    maskt[:, 1024:1280], ALU.mult)))
            tok_mask_sl[sl] = tmax(*mks)
            if with_pv:
                emit_pv(hp, qt)

        def emit_pv(hp, qt):
            sl = hp % 2
            pvm = []
            for hl in range(2):
                h = 2 * hp + hl
                for c in range(3):
                    pvm.append(t_op(lambda e, sl=sl, hl=hl, h=h, c=c, qt=qt: e.matmul(
                        psL[:, h * 64:(h + 1) * 64],
                        etb[sl][:, c * 256 + hl * 128: c * 256 + (hl + 1) * 128],
                        vtm[:, (qt + c) * 1024 + h * 64: (qt + c) * 1024 + (h + 1) * 64],
                        start=(c == 0), stop=(c == 2)),
                        tmax(tok_mask_sl[sl], tok_psL_free, [tok_expg_w], tok_psM_free,
                             [tok_onb])))
                    pvm.append(t_op(lambda e, sl=sl, hl=hl, h=h, c=c: e.matmul(
                        psM[:, h:h + 1],
                        etb[sl][:, c * 256 + hl * 128: c * 256 + (hl + 1) * 128],
                        onesb[:],
                        start=(c == 0), stop=(c == 2))))
            tok_pv_sl[sl] = tmax(*pvm)
            kv_reads.append(tok_pv_sl[sl])
            tok_pv_all.append(tok_pv_sl[sl])

        for qt in range(nqt_run):
            g_tile = half * NQT + qt
            mvar = 0 if g_tile < 2 else 1
            tok_psA_free2 = [tok_psA_free]
            tok_psB_free2 = [tok_psB_free]

            gmm = None
            for dc in range(DC):
                gmm = t_op(lambda e, dc=dc, qt=qt: e.matmul(
                    psM[:, 16:48],
                    qfm[:, dc * 1024 + qt * 128: dc * 1024 + (qt + 1) * 128],
                    gwbdb[:, dc * 32:(dc + 1) * 32],
                    start=(dc == 0), stop=(dc == DC - 1)),
                    tmax([tok_projs], tok_psM_free, [tok_gwbd]))
            kv_reads.append(gmm)
            tok_expg_w = s_op(lambda e: e.activation(expg[:], psM[:, 16:48], AF.Exp,
                                                     bias=0.0, scale=-8.0),
                              tmax([gmm], [tok_gsig_r]))
            v_op(lambda e: e.tensor_scalar(gsig[:, 0:32], expg[:], 1.0, None, ALU.add),
                 [tok_expg_w])
            tok_gsig = v_op(lambda e: e.reciprocal(gsig[:, 0:32], gsig[:, 0:32]))

            tok_pv_all = []
            emit_pair(0, qt, g_tile, mvar, with_pv=False)
            emit_pair(1, qt, g_tile, mvar, with_pv=False)
            if qt > 1:
                part2(qt - 2)
            emit_pv(0, qt)
            emit_pv(1, qt)
            tok_psA_free = tok_psA_free2[0]
            tok_psB_free = tok_psB_free2[0]
            for hp in range(2, 8):
                emit_pair(hp, qt, g_tile, mvar, with_pv=True)
                tok_psA_free = tok_psA_free2[0]
                tok_psB_free = tok_psB_free2[0]
            tok_pv_last = tmax(*tok_pv_all)
            tok_psL_w = tok_pv_last

            lmm = []
            for lh in range(LHN):
                s_ = (lh % 2) * 4 + lh // 2
                lmm.append(t_op(lambda e, lh=lh, qt=qt, s_=s_: e.matmul(
                    psA[:, (lh % 2) * 512 + (lh // 2) * 64: (lh % 2) * 512 + (lh // 2) * 64 + 64],
                    gqfm[(lh % 2) * 64:(lh % 2) * 64 + 64,
                         (lh // 2) * 1024 + qt * 128: (lh // 2) * 1024 + (qt + 1) * 128],
                    gkfm[(lh % 2) * 64:(lh % 2) * 64 + 64, (lh // 2) * 64:(lh // 2 + 1) * 64],
                    start=True, stop=True),
                    tmax([tok_projs], tok_psA_free, tok_psB_free, [gates_tok["gke"]])))
            tok_lsc = tmax(*lmm)
            kv_reads.append(tok_lsc)
            tok_expl = tmax(
                s_op(lambda e: e.activation(elb[:, 0:256], psA[:, 0:256], AF.Exp,
                                            bias=0.0, scale=1.0),
                     tmax(tok_lsc, [tok_elb_read])),
                s_op(lambda e: e.activation(elb[:, 256:512], psA[:, 512:768], AF.Exp,
                                            bias=0.0, scale=1.0)))
            tok_lmw = dma(lmaskw[:], lmask_d[g_tile * P:(g_tile + 1) * P, :],
                          waits=[tok_lmw_read])
            mls = [v_op(lambda e: e.tensor_tensor(elb[:], elb[:], lmaskw[:], ALU.mult),
                        tmax([tok_expl], [tok_elnb_read], [tok_lmw]))]
            tok_lmw_read = mls[0]
            v_op(lambda e: e.tensor_reduce(
                rs8[:, 0:8], elb.ap().rearrange("q (l n) -> q l n", n=64), AX.X, ALU.add),
                tmax(*mls))
            v_op(lambda e: e.reciprocal(rs8[:, 8:16], rs8[:, 0:8]))
            nrm = []
            for s_ in range(LHN):
                nrm.append(v_op(lambda e, s_=s_: e.tensor_scalar(
                    elb[:, s_ * 64:(s_ + 1) * 64], elb[:, s_ * 64:(s_ + 1) * 64],
                    rs8[:, 8 + s_:9 + s_], None, ALU.mult)))
            tok_nrm = tmax(*nrm)
            tls = []
            for s_ in range(LHN):
                tls.append(t_op(lambda e, s_=s_: e.transpose(
                    psT[0:64, s_ * 128:(s_ + 1) * 128], elb[:, s_ * 64:(s_ + 1) * 64],
                    identb[:]), tmax(tok_nrm, tok_psT_free)))
            tok_elnb_read = tmax(*tls)
            tok_elb_read = tok_elnb_read
            tok_elmt = v_op(lambda e: e.tensor_copy(out=elmt[0:64, :], in_=psT[0:64, :]),
                            tmax(*tls, [tok_elmt_read]))
            tok_psT_free = [tok_elmt]
            lpv = []
            for lh in range(LHN):
                s_ = (lh % 2) * 4 + lh // 2
                r0 = (lh % 2) * 64
                dst = psA[r0:r0 + 64,
                          (lh % 2) * 512 + (lh // 2) * 128: (lh % 2) * 512 + (lh // 2 + 1) * 128]
                tp = (0, 64) if lh % 2 else None
                lpv.append(t_op(lambda e, lh=lh, s_=s_, dst=dst, tp=tp: e.matmul(
                    dst, gvtm[0:64, lh * 64:(lh + 1) * 64],
                    elmt[0:64, s_ * 128:(s_ + 1) * 128],
                    start=True, stop=True, tile_position=tp),
                    tmax([tok_elmt], [gates_tok["gv"]], tok_psA_free, [tok_expl])))
            tok_lpv = tmax(*lpv)
            tok_elmt_read = tok_lpv
            tok_rem1 = tmax(
                v_op(lambda e: e.tensor_copy(out=remfm[0:64, :], in_=psA[0:64, 0:512]),
                     tmax(tok_lpv, [tok_remfm_read])),
                v_op(lambda e: e.tensor_copy(out=remfm[64:128, :], in_=psA[64:128, 512:1024])))
            rom = []
            for c in range(4):
                rom.append(t_op(lambda e, c=c: e.matmul(
                    psA[:, 0:512], remfm[:, c * 128:(c + 1) * 128],
                    routb[:, c * 1024: c * 1024 + 512],
                    start=(c == 0), stop=(c == 3)),
                    tmax([tok_rem1], tok_rout)))
                rom.append(t_op(lambda e, c=c: e.matmul(
                    psB[:, 0:512], remfm[:, c * 128:(c + 1) * 128],
                    routb[:, c * 1024 + 512: (c + 1) * 1024],
                    start=(c == 0), stop=(c == 3)),
                    tmax(tok_psB_free)))
            tok_rom = tmax(*rom)
            tok_remfm_read = tok_rom

            tok_recr = v_op(lambda e: e.reciprocal(recr[:], psM[:, 0:16]),
                            tmax([tok_pv_last]))
            tok_psM_free = [tok_recr]
            tok_grc = v_op(lambda e: e.tensor_copy(out=gsig[:, 32:48], in_=gsig[:, 16:32]),
                           [tok_gsig])
            tok_sl_v = v_op(lambda e: e.tensor_tensor(gsig[:, 64:80], gsig[:, 0:16],
                                                      recr[:], ALU.mult))
            tok_lse = v_op(lambda e: e.tensor_copy(out=localsb[:], in_=psL[:, 0:1024]),
                           tmax([tok_psL_w], [tok_localsb_read]))
            tok_psL_free = [tok_lse]
            tok_re1 = v_op(lambda e: e.tensor_copy(out=remsb[:, 0:512], in_=psA[:, 0:512]),
                           tmax(tok_rom, [tok_remsb_read]))
            tok_re2 = v_op(lambda e: e.tensor_copy(out=remsb[:, 512:1024], in_=psB[:, 0:512]))
            tok_gt2 = t_op(lambda e: e.transpose(psM[0:96, 128:256], gsig[:, 0:96],
                                                 identf[:]),
                           tmax([tok_sl_v], [tok_grc], [tok_idf], tok_psM_free))
            tok_gte2 = v_op(lambda e: e.tensor_copy(out=gstb[0:96, :], in_=psM[0:96, 128:256]),
                            [tok_gt2])
            tok_psM_free = [tok_gte2, tok_recr]
            exm = []
            for half_i in range(2):
                exm.append(t_op(lambda e, half_i=half_i: e.matmul(
                    psA[:, half_i * 512:(half_i + 1) * 512],
                    gstb[32:48, :], exp16[32:48, half_i * 512:(half_i + 1) * 512],
                    start=True, stop=True),
                    tmax([tok_gte2], [tok_exp16], [tok_re1])))
                exm.append(t_op(lambda e, half_i=half_i: e.matmul(
                    psB[:, half_i * 512:(half_i + 1) * 512],
                    gstb[64:80, :], exp16[64:80, half_i * 512:(half_i + 1) * 512],
                    start=True, stop=True),
                    tmax([tok_gte2], [tok_re2])))
            tok_exm = tmax(*exm)
            tok_ga = v_op(lambda e, qt=qt: e.tensor_tensor(
                gated_b[qt % 3][:], remsb[:], psA[:, 0:1024], ALU.mult),
                tmax(tok_exm, [tok_gb_read[qt % 3]]))
            tok_gsig_r = [tok_ga, tok_gte2]
            tok_remsb_read = tok_ga
            tok_lscale = v_op(lambda e: e.tensor_tensor(
                localsb[:], localsb[:], psB[:, 0:1024], ALU.mult),
                tmax(tok_exm, [tok_lse]))
            tok_psA_free = [tok_ga]
            tok_psB_free = [tok_lscale]
            tok_stt_q[qt % 3] = g_op(lambda e, qt=qt: e.tensor_tensor(
                gated_b[qt % 3][:], gated_b[qt % 3][:], localsb[:], ALU.add),
                tmax([tok_lscale], [tok_ga]))
            tok_localsb_read = tok_stt_q[qt % 3]
        if sec >= 99:
            if nqt_run > 1:
                part2(nqt_run - 2)
            if nqt_run > 0:
                part2(nqt_run - 1)
        tok_prev_kv_read = tmax(*kv_reads)
        tok_loopbuf_read = bar()

    tap("xres_attn", xres[:, 0:4096], tmax([tok_ln2_done]))

    if stage < 3 or stage >= 10:
        tok_ln4_done = bar()
        tok_out = dma(out_d.rearrange("(t q) d -> q t d", q=P),
                      xres.ap().rearrange("q (t d) -> q t d", d=D),
                      waits=tmax(tok_ln4_done))
        final_syncd = [(f"syncd{i}", p.cnt[f"syncd{i}"]) for i in range(NDMASEM)]
        return _finish(nc, p, sems, final_syncd, dbg_outs)
    ffn_bar = bar()
    tern_state["bar"] = ffn_bar

    tok_w1_read = [ffn_bar] * 4
    tok_w2_read = [ffn_bar, ffn_bar]
    tok_psX_free = [tmax(tok_psA_free), tmax(tok_psA_free),
                    tmax(tok_psB_free), tmax(tok_psB_free)]
    tok_psY_free = [tmax(tok_psL_free), tmax(tok_psL_free)]
    tok_psTf_free = ffn_bar
    tok_psT_free2 = tmax(tok_psT_free)
    tok_h_read = [None, None]
    tok_xd_read = [None, None]
    tok_xl3_read = None
    tok_ftm_read = None
    tok_ln3tmp_read = None
    tok_ln4_done = None

    FDUMP = fdump
    ln3_toks = []
    for t in range(NT):
        tok_s3 = v_op(lambda e, t=t: e.tensor_reduce(
            smv[:, 36 + t:37 + t], xres[:, t * D:(t + 1) * D], AX.X, ALU.add),
            tmax([tok_ln2_done], ffn_bar))
        tok_q3 = s_op(lambda e, t=t: e.activation(
            FDUMP[:, 0:1024], xres[:, t * D:(t + 1) * D], AF.Square, bias=0.0, scale=1.0,
            accum_out=smv[:, 52 + t:53 + t]), tmax([tok_ln2_done], [tok_s3], ffn_bar))
        ln3_toks.append(tok_q3)
    tok_st3 = tmax(*ln3_toks)
    v_op(lambda e: e.tensor_scalar(smv[:, 36:52], smv[:, 36:52], -1.0 / D, None, ALU.mult),
         tok_st3)
    v_op(lambda e: e.tensor_scalar(smv[:, 52:68], smv[:, 52:68], 1.0 / D, None, ALU.mult))
    v_op(lambda e: e.tensor_tensor(smv[:, 68:84], smv[:, 36:52], smv[:, 36:52], ALU.mult))
    tok_v3 = v_op(lambda e: e.tensor_tensor(smv[:, 52:68], smv[:, 52:68],
                                            smv[:, 68:84], ALU.subtract))
    s_op(lambda e: e.activation(smv[:, 52:68], smv[:, 52:68], AF.Ln, bias=epsc[:], scale=1.0),
         [tok_v3])
    tok_r3 = s_op(lambda e: e.activation(smv[:, 52:68], smv[:, 52:68], AF.Exp,
                                         bias=0.0, scale=-0.5))

    def build_xl3(g):
        nonlocal tok_psT_free2, tok_ln3tmp_read
        xl3_toks = []
        prev_tp = tok_psT_free2
        xl3b = xln3fm[g % 2]
        for i in range(4):
            tile = g * 4 + i
            tok_ap = v_op(lambda e, t=tile: e.tensor_scalar(
                ln3tmp[:], xres[:, t * D:(t + 1) * D], smv[:, 36 + t:37 + t],
                smv[:, 52 + t:53 + t], ALU.add, ALU.mult),
                tmax([tok_r3], [tok_ln3tmp_read]))
            tps = []
            for dc in range(DC):
                tps.append(t_op(lambda e, dc=dc: e.transpose(
                    psT[:, dc * 128:(dc + 1) * 128], ln3tmp[:, dc * 128:(dc + 1) * 128],
                    identb[:]), tmax([tok_ap], prev_tp)))
            tok_ln3tmp_read = tmax(*tps)
            dst = xl3b.ap().rearrange("q (c j) -> q c j", j=512)[:, :, i * 128:(i + 1) * 128]
            tok_ev = v_op(lambda e, dst=dst: e.tensor_copy(
                out=dst, in_=psT.ap().rearrange("q (c j) -> q c j", j=128)),
                tmax(*tps, [tok_xl3_read]))
            prev_tp = tok_ev
            xl3_toks.append(tok_ev)
        tok_psT_free2 = tmax(*xl3_toks)
        return tok_psT_free2

    tok_xl3 = build_xl3(0)
    for g in range(NG):
        xl3b = xln3fm[g % 2]

        gelu_toks = []
        mm1_toks = []
        if g == 0:
            emit_tern(3)
        for fc in range(FC):
            if g == 0:
                emit_tern(1 if fc < 28 else 2)
            sl = fc % 4
            bs = fc % 4
            if g == 0:
                tok_w1 = tmax(tern_state["w1s"][fc])
            else:
                srcw = w1tern_d[:, fc * 1024:(fc + 1) * 1024]
                tok_w1 = dma(w1colb[bs][:], srcw,
                             waits=tmax([tok_w1_read[bs]], [tern_state["w1c"][fc]]))
            ps_ap = (psA if sl < 2 else psB)[:, (sl % 2) * 512:(sl % 2 + 1) * 512]
            mm = None
            for dc in range(DC):
                mm = t_op(lambda e, dc=dc, bs=bs, ps_ap=ps_ap, xl3b=xl3b: e.matmul(
                    ps_ap,
                    w1colb[bs][:, dc * 128:(dc + 1) * 128],
                    xl3b[:, dc * 512:(dc + 1) * 512],
                    start=(dc == 0), stop=(dc == DC - 1)),
                    tmax(tok_w1, [tok_xl3], [tok_psX_free[sl]]))
            tok_w1_read[bs] = tmax([mm], [tern_state["w1c"][fc]] if g == 0 else None)
            mm1_toks.append(mm)
            tok_g = s_op(lambda e, fc=fc, g=g, ps_ap=ps_ap: e.activation(
                hbuf[g % 2][:, fc * 512:(fc + 1) * 512], ps_ap,
                AF.Gelu, bias=b1c[:, fc:fc + 1], scale=0.5),
                tmax([mm], [tok_h_read[g % 2]], [tok_b1]))
            tok_psX_free[sl] = tok_g
            gelu_toks.append(tok_g)
        tok_xl3_read = tmax(*mm1_toks)

        tok_xl3_next = build_xl3(g + 1) if g + 1 < NG else None

        ln4_in_toks = []
        for dc in range(DC):
            sl = dc % 2
            py = dc % 2
            if g == 0:
                tok_w2 = tmax(tern_state["w2s"][dc])
            else:
                srcw = w2tern_d[:, dc * 4096:(dc + 1) * 4096]
                tok_w2 = dma(w2colb[sl][:, 0:4096], srcw,
                             waits=tmax([tok_w2_read[sl]], tern_state["w2c"][dc]))
            mm = None
            for fc in range(FC):
                mm = t_op(lambda e, fc=fc, sl=sl, py=py, g=g: e.matmul(
                    psL[:, py * 512:(py + 1) * 512], w2colb[sl][:, fc * 128:(fc + 1) * 128],
                    hbuf[g % 2][:, fc * 512:(fc + 1) * 512],
                    start=(fc == 0), stop=(fc == FC - 1)),
                    tmax(tok_w2, [gelu_toks[fc]], [tok_psY_free[py]]))
            tok_w2_read[sl] = tmax([mm], tern_state["w2c"].get(dc) if g == 0 else None)
            if g == 0:
                emit_tern(4)
            tok_xd = v_op(lambda e, dc=dc, py=py: e.tensor_scalar(
                xdelta[dc % 2][:], psL[:, py * 512:(py + 1) * 512], 0.5, b2c[:, dc:dc + 1],
                ALU.mult, ALU.add),
                tmax([mm], [tok_xd_read[dc % 2]], [tok_b2]))
            tok_psY_free[py] = tok_xd
            tps = []
            for i in range(4):
                tps.append(t_op(lambda e, i=i, dc=dc: e.transpose(
                    psM[:, i * 128:(i + 1) * 128], xdelta[dc % 2][:, i * 128:(i + 1) * 128],
                    identf[:]), tmax([tok_xd], tok_psTf_free, [tok_idf])))
            tok_xd_read[dc % 2] = tmax(*tps)
            dstf = ffn_tm.ap().rearrange("q (t d) -> q t d", d=D)[:, :, dc * 128:(dc + 1) * 128]
            tok_fe = v_op(lambda e, dstf=dstf: e.tensor_copy(
                out=dstf, in_=psM.ap().rearrange("q (t d) -> q t d", d=128)),
                tmax(*tps, [tok_ftm_read]))
            tok_psTf_free = [tok_fe]
            ln4_in_toks.append(tok_fe)
        tok_ftm = tmax(*ln4_in_toks)
        tok_h_read[g % 2] = tmax(tok_w2_read[0], tok_w2_read[1])
        if g == 0:
            assert tern_state["i"] == 64 and len(tern_state["w2c"]) == DC
            tok_h_read[1] = tmax(tok_h_read[1], tern_state["all"],
                                 *tern_state["stg_free"], *tern_state["sa_free"],
                                 *tern_state["sb_free"])

        ftm_reads = []
        for i in range(4):
            tile = g * 4 + i
            c0 = 151 + 4 * (i % 2)
            tok_s4 = v_op(lambda e, i=i, c0=c0: e.tensor_reduce(
                smv[:, c0:c0 + 1], ffn_tm[:, i * D:(i + 1) * D], AX.X, ALU.add), tmax(tok_ftm))
            tok_q4 = s_op(lambda e, i=i, c0=c0: e.activation(
                FDUMP[:, 0:1024], ffn_tm[:, i * D:(i + 1) * D], AF.Square, bias=0.0, scale=1.0,
                accum_out=smv[:, c0 + 1:c0 + 2]), tmax(tok_ftm, [tok_s4]))
            v_op(lambda e, c0=c0: e.tensor_scalar(
                smv[:, c0 + 2:c0 + 3], smv[:, c0:c0 + 1], -1.0 / D, None, ALU.mult), [tok_q4])
            v_op(lambda e, c0=c0: e.tensor_scalar(
                smv[:, c0 + 1:c0 + 2], smv[:, c0 + 1:c0 + 2], 1.0 / D, None, ALU.mult))
            v_op(lambda e, c0=c0: e.tensor_tensor(
                smv[:, 170:171], smv[:, c0 + 2:c0 + 3], smv[:, c0 + 2:c0 + 3], ALU.mult))
            tok_v4 = v_op(lambda e, c0=c0: e.tensor_tensor(
                smv[:, c0 + 1:c0 + 2], smv[:, c0 + 1:c0 + 2], smv[:, 170:171], ALU.subtract))
            s_op(lambda e, c0=c0: e.activation(
                smv[:, c0 + 3:c0 + 4], smv[:, c0 + 1:c0 + 2], AF.Ln, bias=epsc[:], scale=1.0),
                [tok_v4])
            tok_r4 = s_op(lambda e, c0=c0: e.activation(
                smv[:, c0 + 3:c0 + 4], smv[:, c0 + 3:c0 + 4], AF.Exp, bias=0.0, scale=-0.5))
            tok_ap4 = v_op(lambda e, i=i, c0=c0: e.tensor_scalar(
                ftmp[:], ffn_tm[:, i * D:(i + 1) * D], smv[:, c0 + 2:c0 + 3],
                smv[:, c0 + 3:c0 + 4], ALU.add, ALU.mult),
                tmax([tok_r4], [tok_ln4_done]))
            tok_ln4 = g_op(lambda e, t=tile: e.tensor_tensor(
                xres[:, t * D:(t + 1) * D], xres[:, t * D:(t + 1) * D], ftmp[:], ALU.add),
                [tok_ap4])
            ftm_reads.append(tok_ap4)
            tok_ln4_done = tok_ln4
        tok_ftm_read = tmax(*ftm_reads)
        tok_xl3 = tok_xl3_next

    tok_out = dma(out_d.rearrange("(t q) d -> q t d", q=P),
                  xres.ap().rearrange("q (t d) -> q t d", d=D),
                  waits=tmax([tok_ln4_done]))
    final_syncd = [(f"syncd{i}", p.cnt[f"syncd{i}"]) for i in range(NDMASEM)]

    return _finish(nc, p, sems, final_syncd, dbg_outs)


def _finish(nc, p, sems, final_syncd, dbg_outs):
    for c in CH_STREAM:
        sems[c] = nc.semaphore(c + "_sem").__enter__()

    with nc.Block() as block:
        def run(eng_name, eng):
            for (ws, fn, channel, inc) in p.ops[eng_name]:
                for (ch, v) in ws:
                    eng.wait_ge(sems[ch], v)
                ins = fn(eng)
                if inc:
                    ins.then_inc(sems[channel], inc)
            if eng_name == "sync":
                for (ch, v) in final_syncd:
                    if v > 0:
                        eng.wait_ge(sems[ch], v)

        @block.sync
        def _(sync):
            run("sync", sync)

        @block.tensor
        def _(tensor):
            run("tensor", tensor)

        @block.vector
        def _(vector):
            run("vector", vector)

        @block.scalar
        def _(scalar):
            run("scalar", scalar)

        @block.gpsimd
        def _(gpsimd):
            run("gpsimd", gpsimd)

    return nc, dbg_outs


_CACHE = {}


def _get_nc(debug=False, stage=3):
    key = ("nc", debug, stage)
    if key not in _CACHE:
        _CACHE[key] = build_nc(debug, stage)
    return _CACHE[key]


def _host_consts():
    if "consts" in _CACHE:
        return _CACHE["consts"]
    identb = np.eye(P, dtype=np.float32).astype(ml_dtypes.bfloat16)
    identf = np.eye(P, dtype=np.float32)
    onesb = np.ones((P, 1), np.float32).astype(ml_dtypes.bfloat16)
    onesf = np.ones((P, P), np.float32)
    poolv = np.full((P, 1), 1.0 / 128.0, np.float32).astype(ml_dtypes.bfloat16)
    pp, qq = np.meshgrid(np.arange(P), np.arange(P), indexing="ij")
    m_c0 = (qq <= pp - 1).astype(np.float32)
    m_c2 = (qq >= pp).astype(np.float32)
    zz = np.zeros((P, P), np.float32)
    def pairs(c0, c2):
        return np.concatenate([c0, c0, c2, c2], axis=1)
    generic = pairs(m_c0, m_c2)
    var0 = pairs(zz, m_c2)
    var1 = pairs(zz, m_c2)
    e16 = np.zeros((P, 1024), np.float32)
    for base in (0, 32, 64):
        for h in range(16):
            e16[base + h, 64 * h:64 * (h + 1)] = 1.0
    _CACHE["consts"] = dict(identb=identb, identf=identf, onesb=onesb, onesf=onesf,
                            poolv=poolv, generic=generic, var0=var0, var1=var1,
                            exp16=e16.astype(ml_dtypes.bfloat16))
    return _CACHE["consts"]


def _pack_rout(r):
    out = np.zeros((P, 4 * D), np.float32)
    rr = r.reshape(8, 64, D)
    for lh in range(8):
        out[(lh % 2) * 64:(lh % 2) * 64 + 64, (lh // 2) * D:(lh // 2 + 1) * D] = rr[lh]
    return out


def kernel(L0_attn_x, L0_attn_wq, L0_attn_wk, L0_attn_wv, L0_attn_wgq, L0_attn_wgk,
           L0_attn_wgv, L0_attn_r_out_w, L0_attn_gate_local_w, L0_attn_gate_remote_w,
           L0_attn_w_out, L0_ffn_W1, L0_ffn_b1, L0_ffn_W2, L0_ffn_b2, _debug=False,
           _stage=3):
    nc, dbg_outs = _get_nc(_debug, _stage)
    C = _host_consts()
    bf16 = ml_dtypes.bfloat16
    f32 = np.float32

    x = np.ascontiguousarray(L0_attn_x, dtype=f32)
    gwbd = np.zeros((D, 32), f32)
    for h in range(H):
        gwbd[h * 64:(h + 1) * 64, h] = L0_attn_gate_local_w[h]
        gwbd[h * 64:(h + 1) * 64, 16 + h] = L0_attn_gate_remote_w[h]
    w1t = np.ascontiguousarray(
        np.asarray(L0_ffn_W1, f32).T.reshape(8, 128, 32, 128)
        .transpose(1, 2, 0, 3).reshape(128, 32768))
    w2t = np.ascontiguousarray(
        np.asarray(L0_ffn_W2, f32).T.reshape(4, 8, 128, 8, 128)
        .transpose(2, 3, 0, 1, 4).reshape(128, 32768))

    def _colmajor(w, noc):
        w = np.asarray(w, f32)
        return np.ascontiguousarray(
            w.reshape(8, 128, noc, 128).transpose(1, 2, 0, 3).reshape(128, noc * 1024))
    b1r = np.ascontiguousarray(np.asarray(L0_ffn_b1, f32).reshape(FC, P).T)
    b2r = np.ascontiguousarray(np.asarray(L0_ffn_b2, f32).reshape(DC, P).T)

    shared = dict(
        wq=_colmajor(L0_attn_wq, 8), wk=_colmajor(L0_attn_wk, 8),
        wv=_colmajor(L0_attn_wv, 8), wgq=_colmajor(L0_attn_wgq, 4),
        wgk=_colmajor(L0_attn_wgk, 4), wgv=np.ascontiguousarray(L0_attn_wgv, f32),
        r_out=_pack_rout(np.asarray(L0_attn_r_out_w, f32)),
        w_out=np.ascontiguousarray(L0_attn_w_out, f32),
        gwbd=gwbd, w1t=w1t, w2t=w2t, b1r=b1r, b2r=b2r,
        identb=C["identb"], identf=C["identf"], onesb=C["onesb"], onesf=C["onesf"],
        poolv=C["poolv"], exp16=C["exp16"],
    )

    in_maps = []
    for k in range(8):
        b, q = k // 4, k % 4
        xin = np.zeros((T + 256, D), f32)
        if q > 0:
            xin[0:256] = x[b, q * T - 256: q * T]
        xin[256:] = x[b, q * T: (q + 1) * T]
        ones_pair = np.ones((P, 256), np.float32)
        if q == 0:
            maskt = np.concatenate([C["var0"], C["generic"], 0 * ones_pair], axis=1)
        else:
            maskt = np.concatenate([C["generic"], C["generic"], ones_pair], axis=1)
        lm = np.zeros((NT, P, 8, 64), f32)
        for i in range(NT):
            c = q * NT + i
            lm[i, :, :, 0:c + 1] = 1.0
        m = dict(shared)
        m["xin"] = xin
        m["maskt"] = maskt.astype(bf16)
        m["lmask"] = lm.reshape(NT * P, 512).astype(bf16)
        m["bsel"] = np.full((P, 1), float(b), f32)
        in_maps.append(m)

    res = run_bass_kernel_spmd(nc, in_maps, core_ids=list(range(8)))
    out = np.empty((2, 8192, D), f32)
    for k in range(8):
        b, q = k // 4, k % 4
        out[b, q * T:(q + 1) * T] = res.results[k]["out"]
    if _debug:
        return out, [ {n: r[n] for n in dbg_outs if n in r} for r in
                      [res.results[k] for k in range(8)] ]
    return out
